# version 15
# baseline (speedup 1.0000x reference)
import numpy as np
import ml_dtypes
import concourse.bass as bass
import concourse.mybir as mybir
from concourse.bass_utils import run_bass_kernel_spmd
from contextlib import ExitStack

F32 = mybir.dt.float32
BF16 = mybir.dt.bfloat16
AF = mybir.ActivationFunctionType
ALU = mybir.AluOpType
AX = mybir.AxisListType.X

P = 128
S = 2048
D = 1024
H = 4096
E = 8
NDT = D // P
NJC = H // P
TC = 1024
CAP = 384
NCH = S // TC
NTS = TC // P

SIM_MODE = False

ENGS = ("tensor", "vector", "scalar", "gpsimd", "sync")
SEMS = ("wd0", "wd1", "we0", "we1", "wm0", "wm1", "wm2", "wm3", "wm4", "wm5", "wc0", "wc1", "wc2", "wc3", "xt0", "xt1", "xg0", "xg1", "sct", "gt0", "gt1", "xdma", "cdma", "mm", "act", "dve", "pool")
OWN = {"tensor": "mm", "vector": "dve", "scalar": "act", "gpsimd": "pool"}


class Sched:
    def __init__(self):
        self.c = {k: 0 for k in SEMS}
        self.ops = {k: [] for k in ENGS}

    def emit(self, eng, fn, waits=None, inc=None):
        own = OWN.get(eng)
        w = {k: v for k, v in (waits or {}).items() if v > 0}
        if eng == "tensor":
            if SIM_MODE:
                w["mm"] = self.c["mm"]
            else:
                w.pop("mm", None)
        elif own is not None:
            w[own] = self.c[own]
            if inc is None:
                inc = (own, 1)
        self.ops[eng].append((w, inc, fn))
        if inc:
            self.c[inc[0]] += inc[1]
            return self.c[inc[0]]
        return None

    def barrier(self):
        snap = dict(self.c)
        for eng in ("tensor", "vector", "scalar", "gpsimd"):
            w = {s: v for s, v in snap.items() if s != OWN[eng]}
            self.emit(eng, lambda g: g.nop(), waits=w, inc=(OWN[eng], 1))
        snap2 = dict(self.c)
        for eng in ("tensor", "vector", "scalar", "gpsimd", "sync"):
            w = {s: snap2[s] for s in SEMS}
            self.emit(eng, lambda g: g.nop(), waits=w)


def build_program(debug=False):
    nc = bass.Bass("TRN2", target_bir_lowering=False)

    inp = nc.dram_tensor("inp", [S, D], F32, kind="ExternalInput").ap()
    rw_d = nc.dram_tensor("rw", [E, D], F32, kind="ExternalInput").ap()
    rb_d = nc.dram_tensor("rb", [1, E], F32, kind="ExternalInput").ap()
    whg_d = nc.dram_tensor("whg", [E, 2 * D, D], F32, kind="ExternalInput").ap()
    wout_d = nc.dram_tensor("wout", [E, D, D], F32, kind="ExternalInput").ap()
    w1_d = nc.dram_tensor("w1", [E, D, H], F32, kind="ExternalInput").ap()
    w2_d = nc.dram_tensor("w2", [E, H, D], F32, kind="ExternalInput").ap()
    nw8_d = nc.dram_tensor("nw8h", [P, NDT], F32, kind="ExternalInput").ap()
    wr64_d = nc.dram_tensor("wr64h", [P, NDT * E], F32, kind="ExternalInput").ap()
    b1a_d = nc.dram_tensor("b1ah", [P, E * NJC], F32, kind="ExternalInput").ap()
    b2a_d = nc.dram_tensor("b2ah", [P, E * NDT], F32, kind="ExternalInput").ap()
    ident_d = nc.dram_tensor("ident", [P, P], F32, kind="ExternalInput").ap()
    esel_d = nc.dram_tensor("esel", [E, E * P], F32, kind="ExternalInput").ap()
    onesr_d = nc.dram_tensor("onesr", [1, P], F32, kind="ExternalInput").ap()
    onescf_d = nc.dram_tensor("onescf", [P, 1], F32, kind="ExternalInput").ap()
    cvals_d = nc.dram_tensor("cvals", [P, 2], F32, kind="ExternalInput").ap()
    ecol_d = nc.dram_tensor("ecolh", [P, E], F32, kind="ExternalInput").ap()
    eclim_d = nc.dram_tensor("eclimh", [P, E], F32, kind="ExternalInput").ap()
    out_d = nc.dram_tensor("out", [S, D], F32, kind="ExternalOutput").ap()
    xk = "ExternalOutput" if debug else "Internal"
    xfA = nc.dram_tensor("xfA", [D, S], F32, kind=xk).ap()
    xfB = nc.dram_tensor("xfB", [D, S], F32, kind=xk).ap()
    w1c_d = nc.dram_tensor("w1c", [E, D, H], BF16, kind="Internal").ap()
    w2c_d = nc.dram_tensor("w2c", [E, H, D], BF16, kind="Internal").ap()
    slotbuf0 = nc.dram_tensor("slotbuf0", [E * CAP, D], F32, kind="Internal").ap()
    slotbuf1 = nc.dram_tensor("slotbuf1", [E * CAP, D], F32, kind="Internal").ap()
    slotout0 = nc.dram_tensor("slotout0", [E * CAP, D], F32, kind="Internal").ap()
    slotout1 = nc.dram_tensor("slotout1", [E * CAP, D], F32, kind="Internal").ap()

    K = Sched()
    CACHETOT = {}

    with ExitStack() as top:
        def sb(name, shape, dt):
            return top.enter_context(nc.sbuf_tensor(name, shape, dt))

        sems = {s: top.enter_context(nc.semaphore(s)) for s in SEMS}
        ps = [top.enter_context(nc.psum_tensor(f"ps{i}", [P, 1024], F32)) for i in range(4)]

        ident = sb("identS", [P, P], F32)
        esel = sb("eselS", [E, E * P], F32)
        onesr = sb("onesrS", [1, P], F32)
        onescf = sb("onescfS", [P, 1], F32)
        onescb = sb("onescbS", [P, 1], BF16)
        nw8 = sb("nw8", [P, NDT], F32)
        rwraw = sb("rwraw", [E, D], F32)
        rwT = sb("rwT", [P, NDT * E], F32)
        rbrow = sb("rbrow", [1, E], F32)
        wr64 = sb("wr64", [P, NDT * E], F32)
        b1all = sb("b1all", [P, E * NJC], F32)
        b2all = sb("b2all", [P, E * NDT], F32)
        wsumb = sb("wsumb", [P, E], F32)
        probrow = sb("probrow", [1, E], F32)
        pcols = sb("pcols", [P, E], F32)
        r8a = sb("r8a", [1, E], F32)
        r8b = sb("r8b", [1, E], F32)
        tiny = sb("tinyS", [E, 1], F32)
        ecol = sb("ecolS", [P, E], F32)
        eclim = sb("eclimS", [P, E], F32)
        cvals = sb("cvalsS", [P, 2], F32)
        nc.const_aps.register(F32, 1e-5, cvals[:, 0:1])
        nc.const_aps.register(F32, 1e-6, cvals[:, 1:2])

        zscope = ExitStack()
        zbf = zscope.enter_context(nc.sbuf_tensor("zbfS", [P, D], F32))
        for (dst, src) in [(ident[:, :], ident_d[:, :]), (esel[:, :], esel_d[:, :]),
                           (onesr[:, :], onesr_d[:, :]), (onescf[:, :], onescf_d[:, :]),
                           (cvals[:, :], cvals_d[:, :]),
                           (rwraw[:, :], rw_d[:, :]),
                           (rbrow[:, :], rb_d[:, :]), (nw8[:, :], nw8_d[:, :]),
                           (wr64[:, :], wr64_d[:, :]), (b1all[:, :], b1a_d[:, :]),
                           (b2all[:, :], b2a_d[:, :]),
                           (ecol[:, :], ecol_d[:, :]), (eclim[:, :], eclim_d[:, :])]:
            K.emit("gpsimd", (lambda d, s: lambda g: g.dma_start(out=d, in_=s))(dst, src),
                   inc=("cdma", 16))
        cload = K.c["cdma"]
        dz = K.emit("vector", lambda g: g.memset(zbf[:, :], 0.0), inc=("dve", 1))
        for sbd in (slotbuf0, slotbuf1):
            for k in range(E * CAP // P):
                K.emit("gpsimd", (lambda d2, kk: lambda g: g.dma_start(
                    out=d2[kk * P:(kk + 1) * P, :], in_=zbf[:, :]))(sbd, k),
                    waits={"dve": dz}, inc=("cdma", 16))
        cload = K.c["cdma"]
        zscope.close()
        K.emit("vector", lambda g: g.tensor_copy(onescb[:, :], onescf[:, :]),
               waits={"cdma": cload}, inc=("dve", 1))
        K.barrier()

        for dt_i in range(NDT):
            mv = K.emit("tensor", (lambda i: lambda g: g.transpose(
                out=ps[i % 2][0:P, 0:8], in_=rwraw[0:E, i * P:(i + 1) * P],
                identity=ident[0:E, 0:E]))(dt_i),
                waits={"cdma": cload, "dve": K.c["dve"]}, inc=("mm", 1))
            K.emit("vector", (lambda i: lambda g: g.tensor_copy(
                rwT[:, i * 8:(i + 1) * 8], ps[i % 2][0:P, 0:8]))(dt_i),
                waits={"mm": mv}, inc=("dve", 1))
        for dt_i in range(NDT):
            mv = K.emit("tensor", (lambda i: lambda g: g.matmul(
                out=ps[2][0:E, 0:1], lhsT=wr64[:, i * 8:(i + 1) * 8], rhs=onescf[:, :],
                start=(i == 0), stop=(i == NDT - 1)))(dt_i),
                waits={"cdma": cload}, inc=("mm", 1) if dt_i == NDT - 1 else None)
        dv = K.emit("vector", lambda g: g.tensor_copy(tiny[:, :], ps[2][0:E, 0:1]),
                    waits={"mm": mv}, inc=("dve", 1))
        mv = K.emit("tensor", lambda g: g.transpose(out=ps[3][0:1, 0:E], in_=tiny[:, :],
                                                    identity=ident[0:E, 0:E]),
                    waits={"dve": dv}, inc=("mm", 1))
        dv = K.emit("vector", lambda g: g.tensor_copy(r8a[:, :], ps[3][0:1, 0:E]),
                    waits={"mm": mv}, inc=("dve", 1))
        dv = K.emit("vector", lambda g: g.tensor_scalar(
            out=r8a[:, :], in0=r8a[:, :], scalar1=-1.0, scalar2=None, op0=ALU.mult),
            inc=("dve", 1))
        mv = K.emit("tensor", lambda g: g.matmul(out=ps[3][0:P, 0:E], lhsT=onesr[:, :],
                                                 rhs=r8a[:, :], start=True, stop=True),
                    waits={"dve": dv}, inc=("mm", 1))
        K.emit("vector", lambda g: g.tensor_copy(wsumb[:, :], ps[3][0:P, 0:E]),
               waits={"mm": mv}, inc=("dve", 1))
        K.barrier()

        def ffn_call(src_xf, dst_xf, src_ready, tag, src_is_tm=False, dst_is_tm=False):
            slotb = (slotbuf0, slotbuf1)
            sloto = (slotout0, slotout1)
            with ExitStack() as fs:
                def fsb(name, shape, dt):
                    return fs.enter_context(nc.sbuf_tensor(name + tag, shape, dt))
                xc = fsb("xc", [P, 2 * NDT * TC], F32)
                u1tm = fsb("u1tm", [P, NTS * D], F32)
                hsb = fsb("hsb", [P, NJC * CAP], BF16)
                w1b = fsb("w1b", [P, 2 * 2048], BF16)
                w2b = fsb("w2b", [P, 2 * 1024], BF16)
                xg = fsb("xg", [P, 2 * D], F32)
                xg_f = fsb("xg_f", [P, NDT * CAP], BF16)
                outf = fsb("outf", [P, NDT * CAP], F32)
                tmst = fsb("tmst", [P, 2 * D], F32)
                tmpu = fsb("tmpu", [P, D], F32)
                raw_tm = fsb("raw_tm", [P, NTS * E], F32)
                lg_tm = fsb("lg_tm", [P, NTS * E], F32)
                t1_tm = fsb("t1_tm", [P, NTS * E], F32)
                eq_tm = fsb("eq_tm", [P, NTS * E], F32)
                msk_tm = fsb("msk_tm", [P, NTS * E], F32)
                ge_tm = fsb("ge_tm", [P, NTS * E], F32)
                ex_tm = fsb("ex_tm", [P, NTS * E], F32)
                wg_tm = fsb("wg_tm", [P, NTS * E], F32)
                pfx_tm = fsb("pfx_tm", [P, NTS * E], F32)
                sid_tm = fsb("sid_tm", [P, NTS * E], F32)
                c8a = fsb("c8a", [P, NTS], F32)
                c8b = fsb("c8b", [P, NTS], F32)
                c8c = fsb("c8c", [P, NTS], F32)
                c8d = fsb("c8d", [P, NTS], F32)
                c8e = fsb("c8e", [P, NTS], F32)
                m_tm = fsb("m_tm", [P, NTS], F32)
                nr_tm = fsb("nr_tm", [P, NTS], F32)
                wg1c = fsb("wg1c", [P, 2 * NTS], F32)
                wg2c = fsb("wg2c", [P, 2 * NTS], F32)
                idx1u = fsb("idx1u", [P, 2 * NTS], mybir.dt.uint32)
                idx2u = fsb("idx2u", [P, 2 * NTS], mybir.dt.uint32)
                maskT = fsb("maskT", [E, TC], F32)
                ones8 = fsb("ones8", [E, TC], F32)
                pfxT = fsb("pfxT", [E, TC], F32)

                w1st_rel = [0, 0]; w1b_rel = [0, 0]
                w2st_rel = [0, 0]; w2b_rel = [0, 0]
                wdsem = ["wd0", "wd1"]; wesem = ["we0", "we1"]
                xgsem = ["xg0", "xg1"]
                xg_rel = [0, 0]
                gb_rel = [("dve", 0), ("dve", 0)]
                ps_b0 = [("dve", 0)] * 4
                b1_rel = [("dve", 0), ("dve", 0)]
                store_ms = src_ready
                xtsem = ["xt0", "xt1"]
                tmst_rel = [("dve", 0), ("dve", 0)]

                K.emit("vector", lambda g: g.memset(ones8[:, :], 1.0), inc=("dve", 1))

                def addw(w, sem, val):
                    if sem == "dve":
                        w["dve"] = max(w.get("dve", 0), val)
                    else:
                        w[sem] = max(w.get(sem, 0), val)

                def tp_assemble(srcs, dst_aps, n_per, tp_waits=None, copy_waits=None,
                                add_to=None, ts_col=None, op_rows=P, in_rows=P):
                    nh = len(srcs) // 2
                    mm_last = [0, 0]
                    for i, in_ap in enumerate(srcs):
                        j2 = i // nh
                        w = dict(tp_waits or {})
                        addw(w, b1_rel[j2][0], b1_rel[j2][1])
                        mm_last[j2] = K.emit("tensor", (lambda a, k, j3: lambda g: g.transpose(
                            out=ps[2 + j3][0:op_rows, 512 + k * n_per:512 + (k + 1) * n_per],
                            in_=a, identity=ident[0:in_rows, 0:in_rows]))(in_ap, i % nh, j2),
                            waits=w, inc=("mm", 1))
                    last = 0
                    for j2 in range(2):
                        w = dict(copy_waits or {})
                        w["mm"] = max(w.get("mm", 0), mm_last[j2])
                        if add_to is not None:
                            dst = add_to[:, nh * j2:nh * (j2 + 1), ts_col * P:(ts_col + 1) * P]
                            last = K.emit("vector", (lambda j3, d4: lambda g: g.tensor_tensor(
                                out=d4, in0=d4,
                                in1=ps[2 + j3][:, 512:512 + 4 * P].rearrange(
                                    "p (k c) -> p k c", c=P),
                                op=ALU.add))(j2, dst),
                                waits=w, inc=("dve", 1))
                        else:
                            last = K.emit("vector", (lambda j3, d4, np2: lambda g: g.tensor_copy(
                                d4, ps[2 + j3][0:op_rows, 512:512 + nh * np2].rearrange(
                                    "p (k c) -> p k c", c=np2)))(j2, dst_aps[j2], n_per),
                                waits=w, inc=("dve", 1))
                        b1_rel[j2] = ("dve", last)
                    return last

                stref = [src_ready]
                cache_ms = {}
                cache_ring = [("wc0", 0), ("wc1", 0), ("wc2", 0), ("wc3", 0)]
                cache_i = [0]

                def emit_cache(e):
                    if e >= E:
                        return
                    ms = []
                    for (srcw, dstw, nr) in ((w1_d, w1c_d, D), (w2_d, w2c_d, H)):
                        rs = nr // 8
                        for s8 in range(8):
                            ri = cache_i[0] % 4
                            sem_p, val_p = cache_ring[ri]
                            v = K.emit("gpsimd", (lambda s2, d2, ee, r0, r1: lambda g: g.dma_start(
                                out=d2[ee, r0:r1], in_=s2[ee, r0:r1]))(
                                    srcw, dstw, e, s8 * rs, (s8 + 1) * rs),
                                waits={sem_p: val_p}, inc=(sem_p, 16))
                            cache_ring[ri] = (sem_p, v)
                            cache_i[0] += 1
                        ms.append({s2: v2 for (s2, v2) in cache_ring})
                    cache_ms[e] = ms

                def chunk_head(ch):
                    t0 = ch * TC
                    xco = ch * NDT * TC
                    ixo = ch * NTS
                    sbuf_d = slotb[ch]
                    store_ms = stref[0]
                    sct_prev = K.c["sct"]
                    gt_prev0 = K.c["gt0"]; gt_prev1 = K.c["gt1"]
                    if not src_is_tm:
                        xld = 0
                        for dt_i in range(NDT):
                            xld = K.emit("scalar", (lambda i, tt: lambda g: g.dma_start(
                                out=xc[:, xco + i * TC:xco + (i + 1) * TC],
                                in_=src_xf[i * P:(i + 1) * P, tt:tt + TC]))(dt_i, t0),
                                waits={store_ms[0]: store_ms[1]}, inc=("xdma", 16))
                    u1_done = 0
                    xcv = xc[:, xco:xco + NDT * TC].rearrange("p (dt t) -> p dt t", t=TC)
                    for st in range(NTS):
                        sl = st % 2
                        if src_is_tm:
                            ld = K.emit("scalar", (lambda s_, q, tt: lambda g: g.dma_start(
                                out=tmst[:, q * D:(q + 1) * D],
                                in_=src_xf[tt + s_ * P:tt + (s_ + 1) * P, :]))(st, sl, t0),
                                waits={store_ms[0]: store_ms[1],
                                       tmst_rel[sl][0]: tmst_rel[sl][1]},
                                inc=(xtsem[sl], 16))
                            tpw = {xtsem[sl]: ld}
                            cw = {store_ms[0]: store_ms[1]} if store_ms[1] else {}
                            dv_fill = tp_assemble(
                                [tmst[:, sl * D + i * P:sl * D + (i + 1) * P] for i in range(NDT)],
                                [xcv[:, 0:4, st * P:(st + 1) * P],
                                 xcv[:, 4:8, st * P:(st + 1) * P]],
                                P, tpw, cw)
                            tmst_rel[sl] = ("mm", K.c["mm"])
                            rd_w = {xtsem[sl]: ld}
                        else:
                            dv_fill = tp_assemble(
                                [xc[:, xco + i * TC + st * P:xco + i * TC + (st + 1) * P] for i in range(NDT)],
                                [tmst[:, sl * D:sl * D + 4 * P],
                                 tmst[:, sl * D + 4 * P:sl * D + 8 * P]],
                                P, {"xdma": xld},
                                {tmst_rel[sl][0]: tmst_rel[sl][1]} if tmst_rel[sl][1] else {})
                            tmst_rel[sl] = ("dve", dv_fill)
                            rd_w = {"dve": dv_fill}
                        K.emit("vector", (lambda q, s_: lambda g: g.reduce_sum(
                            c8a[:, s_:s_ + 1], tmst[:, q * D:(q + 1) * D], axis=AX))(sl, st),
                            waits=rd_w, inc=("dve", 1))
                        avq = K.emit("scalar", (lambda q: lambda g: g.activation(
                            out=tmpu[:, 0:D], in_=tmst[:, q * D:(q + 1) * D],
                            func=AF.Square))(sl),
                            waits=rd_w, inc=("act", 1))
                        K.emit("vector", (lambda s_: lambda g: g.reduce_sum(
                            c8b[:, s_:s_ + 1], tmpu[:, 0:D], axis=AX))(st),
                            waits={"act": avq}, inc=("dve", 1))
                        K.emit("vector", (lambda s_: lambda g: g.tensor_scalar(
                            out=m_tm[:, s_:s_ + 1], in0=c8a[:, s_:s_ + 1], scalar1=1.0 / D,
                            scalar2=None, op0=ALU.mult))(st), inc=("dve", 1))
                        K.emit("vector", (lambda s_: lambda g: g.tensor_scalar(
                            out=c8c[:, s_:s_ + 1], in0=c8b[:, s_:s_ + 1], scalar1=1.0 / D,
                            scalar2=None, op0=ALU.mult))(st), inc=("dve", 1))
                        K.emit("vector", (lambda s_: lambda g: g.tensor_tensor(
                            out=c8d[:, s_:s_ + 1], in0=m_tm[:, s_:s_ + 1],
                            in1=m_tm[:, s_:s_ + 1], op=ALU.mult))(st), inc=("dve", 1))
                        dv = K.emit("vector", (lambda s_: lambda g: g.tensor_tensor(
                            out=c8c[:, s_:s_ + 1], in0=c8c[:, s_:s_ + 1],
                            in1=c8d[:, s_:s_ + 1], op=ALU.subtract))(st), inc=("dve", 1))
                        av = K.emit("scalar", (lambda s_: lambda g: g.activation(
                            out=c8d[:, s_:s_ + 1], in_=c8c[:, s_:s_ + 1], func=AF.Sqrt,
                            bias=1e-5))(st), waits={"dve": dv}, inc=("act", 1))
                        dv = K.emit("vector", (lambda s_: lambda g: g.reciprocal(
                            nr_tm[:, s_:s_ + 1], c8d[:, s_:s_ + 1]))(st),
                            waits={"act": av}, inc=("dve", 1))
                        u1_done = K.emit("vector", (lambda q, s_: lambda g: g.tensor_scalar(
                            out=u1tm[:, s_ * D:(s_ + 1) * D], in0=tmst[:, q * D:(q + 1) * D],
                            scalar1=m_tm[:, s_:s_ + 1], scalar2=nr_tm[:, s_:s_ + 1],
                            op0=ALU.subtract, op1=ALU.mult))(sl, st),
                            waits={"sct": sct_prev}, inc=("dve", 1))
                        if not src_is_tm:
                            tmst_rel[sl] = ("dve", u1_done)
                    for ts_i in range(NTS):
                        pj = 2 + ts_i % 2
                        mmv = 0
                        for dt_i in range(NDT):
                            w = {}
                            addw(w, ps_b0[pj][0], ps_b0[pj][1])
                            mmv = K.emit("tensor", (lambda t, i, j: lambda g: g.matmul(
                                out=ps[j][:, 0:8], lhsT=xc[:, xco + i * TC + t * P:xco + i * TC + (t + 1) * P],
                                rhs=wr64[:, i * 8:(i + 1) * 8],
                                start=(i == 0), stop=(i == NDT - 1)))(ts_i, dt_i, pj),
                                waits=w if dt_i == 0 else None,
                                inc=("mm", 1) if dt_i == NDT - 1 else None)
                        dvv = K.emit("vector", (lambda t, j: lambda g: g.tensor_copy(
                            raw_tm[:, t * 8:(t + 1) * 8], ps[j][:, 0:8]))(ts_i, pj),
                            waits={"mm": mmv}, inc=("dve", 1))
                        ps_b0[pj] = ("dve", dvv)
                    def stage(fn_maker, eng="vector", waits=None, incsem="dve"):
                        last = 0
                        for t in range(NTS):
                            last = K.emit(eng, fn_maker(t), waits=waits, inc=(incsem, 1))
                        return last
                    stage(lambda t: lambda g: g.scalar_tensor_tensor(
                        out=t1_tm[:, t * 8:(t + 1) * 8], in0=wsumb[:, :], scalar=m_tm[:, t:t + 1],
                        in1=raw_tm[:, t * 8:(t + 1) * 8], op0=ALU.mult, op1=ALU.add))
                    stage(lambda t: lambda g: g.tensor_scalar(
                        out=lg_tm[:, t * 8:(t + 1) * 8], in0=t1_tm[:, t * 8:(t + 1) * 8],
                        scalar1=nr_tm[:, t:t + 1], scalar2=None, op0=ALU.mult))
                    stage(lambda t: lambda g: g.reduce_max(
                        c8a[:, t:t + 1], lg_tm[:, t * 8:(t + 1) * 8], axis=AX))
                    stage(lambda t: lambda g: g.tensor_scalar(
                        out=c8b[:, t:t + 1], in0=c8a[:, t:t + 1], scalar1=-1.0, scalar2=None,
                        op0=ALU.mult))
                    stage(lambda t: lambda g: g.tensor_scalar(
                        out=eq_tm[:, t * 8:(t + 1) * 8], in0=lg_tm[:, t * 8:(t + 1) * 8],
                        scalar1=c8a[:, t:t + 1], scalar2=None, op0=ALU.is_equal))
                    stage(lambda t: lambda g: g.scalar_tensor_tensor(
                        out=msk_tm[:, t * 8:(t + 1) * 8], in0=eq_tm[:, t * 8:(t + 1) * 8],
                        scalar=-1e30, in1=lg_tm[:, t * 8:(t + 1) * 8], op0=ALU.mult, op1=ALU.add))
                    stage(lambda t: lambda g: g.reduce_max(
                        c8c[:, t:t + 1], msk_tm[:, t * 8:(t + 1) * 8], axis=AX))
                    stage(lambda t: lambda g: g.tensor_scalar(
                        out=ge_tm[:, t * 8:(t + 1) * 8], in0=lg_tm[:, t * 8:(t + 1) * 8],
                        scalar1=c8c[:, t:t + 1], scalar2=None, op0=ALU.is_ge))
                    d_lg = K.c["dve"]
                    a_ex = stage(lambda t: lambda g: g.activation(
                        out=ex_tm[:, t * 8:(t + 1) * 8], in_=lg_tm[:, t * 8:(t + 1) * 8],
                        func=AF.Exp, bias=c8b[:, t:t + 1]), eng="scalar",
                        waits={"dve": d_lg}, incsem="act")
                    stage(lambda t: lambda g: g.reduce_sum(
                        c8d[:, t:t + 1], ex_tm[:, t * 8:(t + 1) * 8], axis=AX),
                        waits={"act": a_ex})
                    stage(lambda t: lambda g: g.reciprocal(c8e[:, t:t + 1], c8d[:, t:t + 1]))
                    stage(lambda t: lambda g: g.tensor_scalar(
                        out=wg_tm[:, t * 8:(t + 1) * 8], in0=ex_tm[:, t * 8:(t + 1) * 8],
                        scalar1=c8e[:, t:t + 1], scalar2=0.5, op0=ALU.mult, op1=ALU.mult))
                    d_wg = stage(lambda t: lambda g: g.tensor_tensor(
                        out=wg_tm[:, t * 8:(t + 1) * 8], in0=wg_tm[:, t * 8:(t + 1) * 8],
                        in1=ge_tm[:, t * 8:(t + 1) * 8], op=ALU.mult))
                    tp_assemble([ge_tm[:, t * 8:(t + 1) * 8] for t in range(NTS)],
                                [maskT[0:E, 0:4 * P], maskT[0:E, 4 * P:8 * P]],
                                P, {"dve": d_wg}, op_rows=E)
                    d_mT = K.c["dve"]
                    d_pT = K.emit("vector", lambda g: g.tensor_tensor_scan(
                        out=pfxT[0:E, :], data0=ones8[0:E, :], data1=maskT[0:E, :],
                        initial=0.0, op0=ALU.mult, op1=ALU.add),
                        waits={"dve": d_mT}, inc=("dve", 1))
                    pfv = pfx_tm[:, :].rearrange("p (k c) -> p k c", c=8)
                    tp_assemble([pfxT[0:E, t * P:(t + 1) * P] for t in range(NTS)],
                                [pfv[:, 0:4, :], pfv[:, 4:8, :]],
                                8, {"dve": d_pT}, in_rows=E)
                    d_pfx = K.c["dve"]
                    stage(lambda t: lambda g: g.tensor_tensor(
                        out=sid_tm[:, t * 8:(t + 1) * 8], in0=pfx_tm[:, t * 8:(t + 1) * 8],
                        in1=ecol[:, :], op=ALU.add), waits={"dve": d_pfx})
                    stage(lambda t: lambda g: g.tensor_tensor(
                        out=sid_tm[:, t * 8:(t + 1) * 8], in0=sid_tm[:, t * 8:(t + 1) * 8],
                        in1=eclim[:, :], op=ALU.min))
                    stage(lambda t: lambda g: g.tensor_tensor(
                        out=msk_tm[:, t * 8:(t + 1) * 8], in0=ge_tm[:, t * 8:(t + 1) * 8],
                        in1=eq_tm[:, t * 8:(t + 1) * 8], op=ALU.subtract))
                    stage(lambda t: lambda g: g.tensor_tensor(
                        out=t1_tm[:, t * 8:(t + 1) * 8], in0=eq_tm[:, t * 8:(t + 1) * 8],
                        in1=sid_tm[:, t * 8:(t + 1) * 8], op=ALU.mult))
                    stage(lambda t: lambda g: g.reduce_sum(
                        c8a[:, t:t + 1], t1_tm[:, t * 8:(t + 1) * 8], axis=AX))
                    stage(lambda t: lambda g: g.tensor_tensor(
                        out=t1_tm[:, t * 8:(t + 1) * 8], in0=msk_tm[:, t * 8:(t + 1) * 8],
                        in1=sid_tm[:, t * 8:(t + 1) * 8], op=ALU.mult))
                    stage(lambda t: lambda g: g.reduce_sum(
                        c8b[:, t:t + 1], t1_tm[:, t * 8:(t + 1) * 8], axis=AX))
                    stage(lambda t: lambda g: g.tensor_tensor(
                        out=t1_tm[:, t * 8:(t + 1) * 8], in0=eq_tm[:, t * 8:(t + 1) * 8],
                        in1=wg_tm[:, t * 8:(t + 1) * 8], op=ALU.mult))
                    stage(lambda t: lambda g: g.reduce_sum(
                        wg1c[:, ixo + t:ixo + t + 1], t1_tm[:, t * 8:(t + 1) * 8], axis=AX))
                    stage(lambda t: lambda g: g.tensor_tensor(
                        out=t1_tm[:, t * 8:(t + 1) * 8], in0=msk_tm[:, t * 8:(t + 1) * 8],
                        in1=wg_tm[:, t * 8:(t + 1) * 8], op=ALU.mult))
                    stage(lambda t: lambda g: g.reduce_sum(
                        wg2c[:, ixo + t:ixo + t + 1], t1_tm[:, t * 8:(t + 1) * 8], axis=AX))
                    stage(lambda t: lambda g: g.tensor_copy(
                        idx1u[:, ixo + t:ixo + t + 1], c8a[:, t:t + 1]),
                        waits={"gt0": gt_prev0, "gt1": gt_prev1})
                    d_idx = stage(lambda t: lambda g: g.tensor_copy(
                        idx2u[:, ixo + t:ixo + t + 1], c8b[:, t:t + 1]))
                    for st in range(NTS):
                        for idxt in (idx1u, idx2u):
                            K.emit("gpsimd", (lambda s_, ix: lambda g: g.indirect_dma_start(
                                out=sbuf_d[:, :], out_offset=bass.IndirectOffsetOnAxis(
                                    ap=ix[:, ixo + s_:ixo + s_ + 1], axis=0),
                                in_=u1tm[:, s_ * D:(s_ + 1) * D], in_offset=None,
                                oob_is_err=False))(st, idxt),
                                waits={"dve": d_idx}, inc=("sct", 16))
                    return K.c["sct"]
                def chunk_body(ch, sct_all):
                    first_chunk_cache = (tag == "A" and ch == 0)
                    t0 = ch * TC
                    xco = ch * NDT * TC
                    ixo = ch * NTS
                    sbuf_d = slotb[ch]
                    sbuf_o = sloto[ch]
                    xcv = xc[:, xco:xco + NDT * TC].rearrange("p (dt t) -> p dt t", t=TC)
                    store_ms = stref[0]
                    gelu_all_prev = 0
                    for e in range(E):
                        if first_chunk_cache:
                            emit_cache(e + 2)
                            cw1 = dict(cache_ms[e][0])
                            cw2 = dict(cache_ms[e][1])
                        else:
                            cw1 = dict(CACHETOT)
                            cw2 = dict(CACHETOT)
                        xg_done = 0
                        xgfv = xg_f[:, :].rearrange("p (dt c) -> p dt c", c=CAP)
                        for k in range(CAP // P):
                            ksl = k % 2
                            gld = K.emit("sync", (lambda ee, kk, q: lambda g: g.dma_start(
                                out=xg[:, q * D:(q + 1) * D],
                                in_=sbuf_d[ee * CAP + kk * P:ee * CAP + (kk + 1) * P, :]))(e, k, ksl),
                                waits={"sct": sct_all, "mm": xg_rel[ksl]},
                                inc=(xgsem[ksl], 16))
                            xg_done = tp_assemble(
                                [xg[:, ksl * D + i * P:ksl * D + (i + 1) * P] for i in range(NDT)],
                                [xgfv[:, 0:4, k * P:(k + 1) * P],
                                 xgfv[:, 4:8, k * P:(k + 1) * P]],
                                P, {xgsem[ksl]: gld}, {"mm": K.c["mm"]})
                            xg_rel[ksl] = K.c["mm"]
                        gelu_ms = {}
                        w1cast = 0
                        for jc in range(NJC):
                            if jc % 2 == 0:
                                pr = (jc // 2) % 2
                                w1cast = K.emit("sync", (lambda ee, j, q: lambda g: g.dma_start(
                                    out=w1b[:, q * 2048:(q + 1) * 2048].rearrange(
                                        "p (dt j) -> p dt j", j=2 * P),
                                    in_=w1c_d[ee].rearrange("(dt p) h -> p dt h", p=P)[:, :, j * P:(j + 2) * P]))(e, jc, pr),
                                    waits=dict(cw1, mm=w1b_rel[pr]), inc=(wdsem[pr], 16))
                                w1st_rel[pr] = w1cast
                            pr = (jc // 2) % 2
                            joff = jc % 2
                            hpr = jc % 2
                            mmv = 0
                            for dt_i in range(NDT):
                                w = {"dve": xg_done, wdsem[pr]: w1cast}
                                if dt_i == 0:
                                    w["act"] = gelu_ms.get(jc - 2, gelu_all_prev)
                                    addw(w, ps_b0[hpr][0], ps_b0[hpr][1])
                                mmv = K.emit("tensor", (lambda q, jo, i, hp: lambda g: g.matmul(
                                    out=ps[hp][:, 0:CAP],
                                    lhsT=w1b[:, q * 2048 + i * 256 + jo * P:
                                             q * 2048 + i * 256 + jo * P + P],
                                    rhs=xg_f[:, i * CAP:(i + 1) * CAP],
                                    start=(i == 0), stop=(i == NDT - 1)))(pr, joff, dt_i, hpr),
                                    waits=w, inc=("mm", 1) if dt_i == NDT - 1 else None)
                            w1b_rel[pr] = mmv
                            gelu_ms[jc] = K.emit("scalar", (lambda ee, j, hp: lambda g: g.activation(
                                out=hsb[:, j * CAP:(j + 1) * CAP], in_=ps[hp][:, 0:CAP],
                                func=AF.Gelu_apprx_tanh,
                                bias=b1all[:, ee * NJC + j:ee * NJC + j + 1]))(e, jc, hpr),
                                waits={"mm": mmv}, inc=("act", 1))
                            ps_b0[hpr] = ("act", gelu_ms[jc])
                        gelu_all = gelu_ms[NJC - 1]
                        gelu_all_prev = gelu_all
                        w2cast = 0
                        ev_ms = 0
                        for half in range(2):
                            for jc in range(NJC):
                                if jc % 2 == 0:
                                    pr = (jc // 2) % 2
                                    w2cast = K.emit("sync", (lambda ee, j, hf, q: lambda g: g.dma_start(
                                        out=w2b[:, q * 1024:(q + 1) * 1024].rearrange(
                                            "p (jj o) -> p jj o", o=512),
                                        in_=w2c_d[ee].rearrange("(jj p) o -> p jj o", p=P)[:, j:j + 2, hf * 512:(hf + 1) * 512]))(e, jc, half, pr),
                                        waits=dict(cw2, mm=w2b_rel[pr]), inc=(wesem[pr], 16))
                                    w2st_rel[pr] = w2cast
                                pr = (jc // 2) % 2
                                joff = jc % 2
                                mmv = 0
                                for oc_rel in range(4):
                                    w = {"act": gelu_ms[jc], wesem[pr]: w2cast}
                                    if jc == 0:
                                        w["act"] = max(w["act"], gelu_all)
                                        addw(w, ps_b0[oc_rel][0], ps_b0[oc_rel][1])
                                    mmv = K.emit("tensor", (lambda j, q, jo, o: lambda g: g.matmul(
                                        out=ps[o][:, 0:CAP],
                                        lhsT=w2b[:, q * 1024 + jo * 512 + o * P:
                                                 q * 1024 + jo * 512 + o * P + P],
                                        rhs=hsb[:, j * CAP:(j + 1) * CAP],
                                        start=(j == 0), stop=(j == NJC - 1)))(jc, pr, joff, oc_rel),
                                        waits=w, inc=("mm", 1) if oc_rel == 3 else None)
                                w2b_rel[pr] = mmv
                            m_p2 = mmv
                            for oc_rel in range(4):
                                oc = half * 4 + oc_rel
                                ev_ms = K.emit("vector", (lambda ee, o, orl: lambda g: g.tensor_scalar(
                                    out=outf[:, o * CAP:(o + 1) * CAP], in0=ps[orl][:, 0:CAP],
                                    scalar1=b2all[:, ee * NDT + o:ee * NDT + o + 1],
                                    scalar2=None, op0=ALU.add))(e, oc, oc_rel),
                                    waits={"mm": m_p2}, inc=("dve", 1))
                                ps_b0[oc_rel] = ("dve", ev_ms)
                        for k in range(CAP // P):
                            sl = k % 2
                            cpw = {tmst_rel[sl][0]: tmst_rel[sl][1]} if tmst_rel[sl][1] else {}
                            cpv = tp_assemble(
                                [outf[:, i * CAP + k * P:i * CAP + (k + 1) * P] for i in range(NDT)],
                                [tmst[:, sl * D:sl * D + 4 * P],
                                 tmst[:, sl * D + 4 * P:sl * D + 8 * P]],
                                P, {"dve": ev_ms}, cpw)
                            K.emit("scalar", (lambda ee, kk, q: lambda g: g.dma_start(
                                out=sbuf_o[ee * CAP + kk * P:ee * CAP + (kk + 1) * P, :],
                                in_=tmst[:, q * D:(q + 1) * D]))(e, k, sl),
                                waits={"dve": cpv}, inc=(xtsem[sl], 16))
                            tmst_rel[sl] = (xtsem[sl], K.c[xtsem[sl]])
                    sout0 = K.c["xt0"]; sout1 = K.c["xt1"]
                    for ts_i in range(NTS):
                        w = {"xt0": sout0, "xt1": sout1}
                        addw(w, gb_rel[0][0], gb_rel[0][1])
                        addw(w, tmst_rel[0][0], tmst_rel[0][1])
                        g1 = K.emit("gpsimd", (lambda t: lambda g: g.indirect_dma_start(
                            out=tmst[:, 0:D], out_offset=None, in_=sbuf_o[:, :],
                            in_offset=bass.IndirectOffsetOnAxis(ap=idx1u[:, ixo + t:ixo + t + 1], axis=0),
                            oob_is_err=False))(ts_i),
                            waits=w, inc=("gt0", 16))
                        w = {"xt0": sout0, "xt1": sout1}
                        addw(w, gb_rel[1][0], gb_rel[1][1])
                        addw(w, tmst_rel[1][0], tmst_rel[1][1])
                        g2 = K.emit("gpsimd", (lambda t: lambda g: g.indirect_dma_start(
                            out=tmst[:, D:2 * D], out_offset=None, in_=sbuf_o[:, :],
                            in_offset=bass.IndirectOffsetOnAxis(ap=idx2u[:, ixo + t:ixo + t + 1], axis=0),
                            oob_is_err=False))(ts_i),
                            waits=w, inc=("gt1", 16))
                        d_t1 = K.emit("vector", (lambda t: lambda g: g.tensor_scalar(
                            out=tmpu[:, 0:D], in0=tmst[:, 0:D], scalar1=wg1c[:, ixo + t:ixo + t + 1],
                            scalar2=None, op0=ALU.mult))(ts_i),
                            waits={"gt0": g1}, inc=("dve", 1))
                        gb_rel[0] = ("dve", d_t1)
                        tmst_rel[0] = ("dve", d_t1)
                        d_cb = K.emit("vector", (lambda t: lambda g: g.scalar_tensor_tensor(
                            out=tmst[:, D:2 * D], in0=tmst[:, D:2 * D], scalar=wg2c[:, ixo + t:ixo + t + 1],
                            in1=tmpu[:, 0:D], op0=ALU.mult, op1=ALU.add))(ts_i),
                            waits={"gt1": g2}, inc=("dve", 1))
                        cb_last = tp_assemble(
                            [tmst[:, D + i * P:D + (i + 1) * P] for i in range(NDT)],
                            [None, None], P, {"dve": d_cb}, None, add_to=xcv, ts_col=ts_i)

                        gb_rel[1] = ("dve", cb_last)
                        tmst_rel[1] = ("dve", cb_last)
                    ev_end = K.c["dve"]
                    if dst_is_tm:
                        for st in range(NTS):
                            sl = st % 2
                            mm_last2 = 0
                            for dt_i in range(NDT):
                                w = {"dve": ev_end}
                                if dt_i == 0:
                                    addw(w, ps_b0[sl][0], ps_b0[sl][1])
                                mm_last2 = K.emit("tensor", (lambda q, i, s_: lambda g: g.transpose(
                                    out=ps[q][:, i * P:(i + 1) * P],
                                    in_=xc[:, xco + i * TC + s_ * P:xco + i * TC + (s_ + 1) * P],
                                    identity=ident[:, :]))(sl, dt_i, st),
                                    waits=w, inc=("mm", 1) if dt_i == NDT - 1 else None)
                            cpv = K.emit("vector", (lambda q: lambda g: g.tensor_copy(
                                tmst[:, q * D:(q + 1) * D], ps[q][:, :]))(sl),
                                waits={"mm": mm_last2, tmst_rel[sl][0]: tmst_rel[sl][1]},
                                inc=("dve", 1))
                            ps_b0[sl] = ("dve", cpv)
                            K.emit("scalar", (lambda q, s_, tt: lambda g: g.dma_start(
                                out=dst_xf[tt + s_ * P:tt + (s_ + 1) * P, :],
                                in_=tmst[:, q * D:(q + 1) * D]))(sl, st, t0),
                                waits={"dve": cpv}, inc=(xtsem[sl], 16))
                            tmst_rel[sl] = (xtsem[sl], K.c[xtsem[sl]])
                        stref[0] = ("mm", mm_last2)
                    else:
                        for dt_i in range(NDT):
                            stref[0] = ("xdma", K.emit("scalar", (lambda i, tt: lambda g: g.dma_start(
                                out=dst_xf[i * P:(i + 1) * P, tt:tt + TC],
                                in_=xc[:, xco + i * TC:xco + (i + 1) * TC]))(dt_i, t0),
                                waits={"dve": ev_end}, inc=("xdma", 16)))

                if tag == "A":
                    emit_cache(0)
                sct_list = [chunk_head(ch2) for ch2 in range(NCH)]
                if tag == "A":
                    emit_cache(1)
                for ch2 in range(NCH):
                    chunk_body(ch2, sct_list[ch2])
                if tag == "A":
                    CACHETOT.update({s: K.c[s] for s in ("wc0", "wc1", "wc2", "wc3")})
                store_ms = stref[0]
            K.barrier()
            return store_ms

        ffn1_store = ffn_call(inp, xfA, ("xdma", 0), "A", src_is_tm=True)

        with ExitStack() as ms:
            def msb(name, shape, dt):
                return ms.enter_context(nc.sbuf_tensor(name, shape, dt))
            mx = msb("mx", [P, NDT * S], F32)
            xn = msb("xn", [P, NDT * S], BF16)
            xnsA = msb("xnsA", [P, NDT], F32)
            xnsB = msb("xnsB", [P, NDT], F32)
            xns = msb("xns", [P, NDT], F32)
            ps_rel_dve = [0, 0, 0, 0]
            ps_rel_act = [0, 0, 0, 0]

            xld = 0
            for dt_i in range(NDT):
                xld = K.emit("scalar", (lambda i: lambda g: g.dma_start(
                    out=mx[:, i * S:(i + 1) * S], in_=xfA[i * P:(i + 1) * P, :]))(dt_i),
                    waits={ffn1_store[0]: ffn1_store[1]}, inc=("xdma", 16))
            ts2 = ExitStack()
            t_t = ts2.enter_context(nc.sbuf_tensor("t_t", [P, NDT * S], F32))
            mrowm = ts2.enter_context(nc.sbuf_tensor("mrowm", [1, S], F32))
            mrowr = ts2.enter_context(nc.sbuf_tensor("mrowr", [1, S], F32))
            mrows1 = ts2.enter_context(nc.sbuf_tensor("mrows1", [1, S], F32))
            bfscr2 = ts2.enter_context(nc.sbuf_tensor("bfscr2", [P, 2 * S], BF16))
            sq_ms = [0, 0]
            for dt_i in range(NDT):
                sl = dt_i % 2
                a1 = K.emit("scalar", (lambda i, q: lambda g: g.activation(
                    out=bfscr2[:, q * S:(q + 1) * S], in_=mx[:, i * S:(i + 1) * S],
                    func=AF.Copy))(dt_i, sl),
                    waits={"xdma": xld, "mm": sq_ms[sl]}, inc=("act", 1))
                mmv = 0
                for sc in range(4):
                    pj = 2 + sc // 2
                    mmv = K.emit("tensor", (lambda q, s_, j, i: lambda g: g.matmul(
                        out=ps[j][0:1, (s_ % 2) * 512:(s_ % 2 + 1) * 512], lhsT=onescb[:, :],
                        rhs=bfscr2[:, q * S + s_ * 512:q * S + (s_ + 1) * 512],
                        start=(i == 0), stop=(i == NDT - 1)))(sl, sc, pj, dt_i),
                        waits={"act": a1}, inc=("mm", 1) if sc == 3 else None)
                a2 = K.emit("scalar", (lambda i, q: lambda g: g.activation(
                    out=bfscr2[:, q * S:(q + 1) * S], in_=mx[:, i * S:(i + 1) * S],
                    func=AF.Square))(dt_i, sl),
                    waits={"mm": mmv}, inc=("act", 1))
                for sc in range(4):
                    pj = sc // 2
                    mmv = K.emit("tensor", (lambda q, s_, j, i: lambda g: g.matmul(
                        out=ps[j][0:1, (s_ % 2) * 512:(s_ % 2 + 1) * 512], lhsT=onescb[:, :],
                        rhs=bfscr2[:, q * S + s_ * 512:q * S + (s_ + 1) * 512],
                        start=(i == 0), stop=(i == NDT - 1)))(sl, sc, pj, dt_i),
                        waits={"act": a2}, inc=("mm", 1) if sc == 3 else None)
                sq_ms[sl] = mmv
            m_sums = K.c["mm"]
            dv = 0
            for hf in range(2):
                dv = K.emit("vector", (lambda h2: lambda g: g.tensor_scalar(
                    out=mrowm[0:1, h2 * 1024:(h2 + 1) * 1024], in0=ps[2 + h2][0:1, 0:1024],
                    scalar1=1.0 / D, scalar2=None, op0=ALU.mult))(hf),
                    waits={"mm": m_sums}, inc=("dve", 1))
            av = K.emit("scalar", lambda g: g.activation(out=mrows1[:, :], in_=mrowm[:, :],
                                                        func=AF.Square),
                        waits={"dve": dv}, inc=("act", 1))
            for hf in range(2):
                dv = K.emit("vector", (lambda h2: lambda g: g.scalar_tensor_tensor(
                    out=mrows1[0:1, h2 * 1024:(h2 + 1) * 1024], in0=ps[h2][0:1, 0:1024],
                    scalar=1.0 / D, in1=mrows1[0:1, h2 * 1024:(h2 + 1) * 1024],
                    op0=ALU.mult, op1=ALU.subtract))(hf),
                    waits={"act": av}, inc=("dve", 1))
            av = K.emit("scalar", lambda g: g.activation(out=mrowr[:, :], in_=mrows1[:, :],
                                                        func=AF.Sqrt, bias=1e-5),
                        waits={"dve": dv}, inc=("act", 1))
            d_rows = K.emit("vector", lambda g: g.reciprocal(mrows1[:, :], mrowr[:, :]),
                            waits={"act": av}, inc=("dve", 1))
            m_bc = 0
            for sc in range(4):
                K.emit("tensor", (lambda s_: lambda g: g.matmul(
                    out=ps[s_ // 2][:, (s_ % 2) * 512:(s_ % 2 + 1) * 512], lhsT=onesr[:, :],
                    rhs=mrowm[0:1, s_ * 512:(s_ + 1) * 512], start=True, stop=True))(sc),
                    waits={"dve": d_rows})
                m_bc = K.emit("tensor", (lambda s_: lambda g: g.matmul(
                    out=ps[2 + s_ // 2][:, (s_ % 2) * 512:(s_ % 2 + 1) * 512], lhsT=onesr[:, :],
                    rhs=mrows1[0:1, s_ * 512:(s_ + 1) * 512], start=True, stop=True))(sc),
                    waits={"dve": d_rows}, inc=("mm", 1))
            t_ms = 0
            for dt_i in range(NDT):
                for hf in range(2):
                    K.emit("vector", (lambda i, h2: lambda g: g.tensor_tensor(
                        out=t_t[:, i * S + h2 * 1024:i * S + (h2 + 1) * 1024],
                        in0=mx[:, i * S + h2 * 1024:i * S + (h2 + 1) * 1024],
                        in1=ps[h2][:, 0:1024], op=ALU.subtract))(dt_i, hf),
                        waits={"mm": m_bc}, inc=("dve", 1))
                    t_ms = K.emit("vector", (lambda i, h2: lambda g: g.tensor_tensor(
                        out=t_t[:, i * S + h2 * 1024:i * S + (h2 + 1) * 1024],
                        in0=t_t[:, i * S + h2 * 1024:i * S + (h2 + 1) * 1024],
                        in1=ps[2 + h2][:, 0:1024], op=ALU.mult))(dt_i, hf), inc=("dve", 1))
            sq_ms = [0, 0]
            mmv = 0
            for dt_i in range(NDT):
                sl = dt_i % 2
                a1 = K.emit("scalar", (lambda i, q: lambda g: g.activation(
                    out=bfscr2[:, q * S:(q + 1) * S], in_=t_t[:, i * S:(i + 1) * S],
                    func=AF.Square))(dt_i, sl),
                    waits={"dve": t_ms, "mm": sq_ms[sl]}, inc=("act", 1))
                for sc in range(4):
                    pj = sc // 2
                    w = {"act": a1}
                    if dt_i == 0 and sc == 0:
                        w["dve"] = t_ms
                    mmv = K.emit("tensor", (lambda q, s_, j, i: lambda g: g.matmul(
                        out=ps[j][0:1, (s_ % 2) * 512:(s_ % 2 + 1) * 512], lhsT=onescb[:, :],
                        rhs=bfscr2[:, q * S + s_ * 512:q * S + (s_ + 1) * 512],
                        start=(i == 0), stop=(i == NDT - 1)))(sl, sc, pj, dt_i),
                        waits=w, inc=("mm", 1) if sc == 3 else None)
                sq_ms[sl] = mmv
            m_tsq = K.c["mm"]
            dv = 0
            for hf in range(2):
                dv = K.emit("vector", (lambda h2: lambda g: g.tensor_scalar(
                    out=mrowr[0:1, h2 * 1024:(h2 + 1) * 1024], in0=ps[h2][0:1, 0:1024],
                    scalar1=1.0 / D, scalar2=None, op0=ALU.mult))(hf),
                    waits={"mm": m_tsq}, inc=("dve", 1))
            av = K.emit("scalar", lambda g: g.activation(out=mrowm[:, :], in_=mrowr[:, :],
                                                        func=AF.Sqrt, bias=1e-6),
                        waits={"dve": dv}, inc=("act", 1))
            dv = K.emit("vector", lambda g: g.reciprocal(mrowr[:, :], mrowm[:, :]),
                        waits={"act": av}, inc=("dve", 1))
            m_rb = 0
            for sc in range(4):
                m_rb = K.emit("tensor", (lambda s_: lambda g: g.matmul(
                    out=ps[s_ // 2][:, (s_ % 2) * 512:(s_ % 2 + 1) * 512], lhsT=onesr[:, :],
                    rhs=mrowr[0:1, s_ * 512:(s_ + 1) * 512], start=True, stop=True))(sc),
                    waits={"dve": dv}, inc=("mm", 1))
            xn_ms = 0
            for dt_i in range(NDT):
                for hf in range(2):
                    acc = xnsA if hf == 0 else xnsB
                    xn_ms = K.emit("vector", (lambda i, h2, a3: lambda g: g.scalar_tensor_tensor(
                        out=xn[:, i * S + h2 * 1024:i * S + (h2 + 1) * 1024],
                        in0=t_t[:, i * S + h2 * 1024:i * S + (h2 + 1) * 1024],
                        scalar=nw8[:, i:i + 1], in1=ps[h2][:, 0:1024],
                        op0=ALU.mult, op1=ALU.mult, accum_out=a3[:, i:i + 1]))(dt_i, hf, acc),
                        waits={"mm": m_rb}, inc=("dve", 1))
            d_xnsum = K.emit("vector", lambda g: g.tensor_tensor(
                out=xns[:, :], in0=xnsA[:, :], in1=xnsB[:, :], op=ALU.add), inc=("dve", 1))
            ps_rel_dve = [xn_ms, xn_ms, t_ms, t_ms]
            for dt_i in range(NDT):
                K.emit("vector", (lambda i: lambda g: g.tensor_tensor(
                    out=mx[:, i * S:(i + 1) * S], in0=mx[:, i * S:(i + 1) * S],
                    in1=t_t[:, i * S:(i + 1) * S], op=ALU.add))(dt_i), inc=("dve", 1))
            ts2.close()
            mmv = 0
            for dt_i in range(NDT):
                mmv = K.emit("tensor", (lambda i: lambda g: g.matmul(
                    out=ps[0][0:1, 0:8], lhsT=xns[:, i:i + 1], rhs=rwT[:, i * 8:(i + 1) * 8],
                    start=(i == 0), stop=(i == NDT - 1)))(dt_i),
                    waits={"dve": max(d_xnsum, ps_rel_dve[0])},
                    inc=("mm", 1) if dt_i == NDT - 1 else None)
            K.emit("vector", lambda g: g.tensor_scalar(
                out=r8a[:, :], in0=ps[0][0:1, 0:8], scalar1=1.0 / S, scalar2=None, op0=ALU.mult),
                waits={"mm": mmv}, inc=("dve", 1))
            K.emit("vector", lambda g: g.tensor_tensor(out=r8a[:, :], in0=r8a[:, :],
                                                       in1=rbrow[:, :], op=ALU.add), inc=("dve", 1))
            K.emit("vector", lambda g: g.reduce_max(r8b[0:1, 0:1], r8a[:, :], axis=AX),
                   inc=("dve", 1))
            dv = K.emit("vector", lambda g: g.tensor_scalar(
                out=r8b[0:1, 1:2], in0=r8b[0:1, 0:1], scalar1=-1.0, scalar2=None, op0=ALU.mult),
                inc=("dve", 1))
            av = K.emit("scalar", lambda g: g.activation(out=probrow[:, :], in_=r8a[:, :],
                                                        func=AF.Exp, bias=r8b[0:1, 1:2]),
                        waits={"dve": dv}, inc=("act", 1))
            K.emit("vector", lambda g: g.reduce_sum(r8b[0:1, 2:3], probrow[:, :], axis=AX),
                   waits={"act": av}, inc=("dve", 1))
            K.emit("vector", lambda g: g.reciprocal(r8b[0:1, 3:4], r8b[0:1, 2:3]), inc=("dve", 1))
            dv = K.emit("vector", lambda g: g.tensor_scalar(
                out=probrow[:, :], in0=probrow[:, :], scalar1=r8b[0:1, 3:4], scalar2=None,
                op0=ALU.mult), inc=("dve", 1))
            mmv = K.emit("tensor", lambda g: g.transpose(out=ps[1][0:8, 0:1], in_=probrow[:, :],
                                                        identity=ident[0:1, 0:1]),
                         waits={"dve": max(dv, ps_rel_dve[1])}, inc=("mm", 1))
            dvp = K.emit("vector", lambda g: g.tensor_copy(tiny[:, :], ps[1][0:8, 0:1]),
                         waits={"mm": mmv}, inc=("dve", 1))
            for e in range(E):
                mmv = K.emit("tensor", (lambda ee: lambda g: g.matmul(
                    out=ps[1][:, 8:9], lhsT=esel[0:E, ee * P:(ee + 1) * P], rhs=tiny[:, :],
                    start=True, stop=True))(e),
                    waits={"dve": K.c["dve"]}, inc=("mm", 1))
                K.emit("vector", (lambda ee: lambda g: g.tensor_copy(
                    pcols[:, ee:ee + 1], ps[1][:, 8:9]))(e),
                    waits={"mm": mmv}, inc=("dve", 1))
            d_pcols = K.c["dve"]

            with ExitStack() as hs2:
                hbf = hs2.enter_context(nc.sbuf_tensor("hbf", [P, NDT * S], BF16))
                hid_t = hs2.enter_context(nc.sbuf_tensor("hid_t", [P, S], F32))
                gat_t = hs2.enter_context(nc.sbuf_tensor("gat_t", [P, S], F32))
                ct_t = hs2.enter_context(nc.sbuf_tensor("ct_t", [P, S], F32))
                macc = hs2.enter_context(nc.sbuf_tensor("macc", [P, 4 * 1024], F32))
                wtT = hs2.enter_context(nc.sbuf_tensor("wtT", [P, NDT * 256], BF16))
                wotT = hs2.enter_context(nc.sbuf_tensor("wotT", [P, NDT * P], BF16))
                wst = hs2.enter_context(nc.sbuf_tensor("wst", [P, 5 * 1024], F32))
                wst_rel = [("dve", d_pcols)] * 5
                wmsem = ["wm0", "wm1", "wm2", "wm3", "wm4"]
                state = {"slot": 0, "hg_done": 0}
                macc_rel = [0, 0]

                def hg_fma(cq):
                    par = cq % 2
                    fma = 0
                    for e in range(E):
                        for part in range(2):
                            slq = state["slot"] % 5; state["slot"] += 1
                            roff = cq * P if part == 0 else D + cq * P
                            wld = K.emit("sync", (lambda ee, ro, q: lambda g: g.dma_start(
                                out=wst[:, q * 1024:(q + 1) * 1024],
                                in_=whg_d[ee, ro:ro + P, :]))(e, roff, slq),
                                waits={wst_rel[slq][0]: wst_rel[slq][1]}, inc=(wmsem[slq], 16))
                            tgt = par * 2048 + (0 if part == 0 else 1024)
                            if e == 0:
                                fma = K.emit("vector", (lambda q, tg, ee: lambda g: g.tensor_scalar(
                                    out=macc[:, tg:tg + 1024], in0=wst[:, q * 1024:(q + 1) * 1024],
                                    scalar1=pcols[:, ee:ee + 1], scalar2=None, op0=ALU.mult))(slq, tgt, e),
                                    waits={wmsem[slq]: wld, "dve": d_pcols, "mm": macc_rel[par]},
                                    inc=("dve", 1))
                            else:
                                fma = K.emit("vector", (lambda q, tg, ee: lambda g: g.scalar_tensor_tensor(
                                    out=macc[:, tg:tg + 1024], in0=wst[:, q * 1024:(q + 1) * 1024],
                                    scalar=pcols[:, ee:ee + 1], in1=macc[:, tg:tg + 1024],
                                    op0=ALU.mult, op1=ALU.add))(slq, tgt, e),
                                    waits={wmsem[slq]: wld}, inc=("dve", 1))
                            wst_rel[slq] = ("dve", fma)
                    return fma

                def hg_tp(cq, fma):
                    par = cq % 2
                    a_tp = 0
                    mm_last = 0
                    for part in range(2):
                        src_off = par * 2048 + (0 if part == 0 else 1024)
                        pj = 2 + part
                        mmv = 0
                        for dt_i in range(NDT):
                            mmv = K.emit("tensor", (lambda so, i, j: lambda g: g.transpose(
                                out=ps[j][:, i * P:(i + 1) * P],
                                in_=macc[:, so + i * P:so + (i + 1) * P],
                                identity=ident[:, :]))(src_off, dt_i, pj),
                                waits={"dve": fma, "act": ps_rel_act[pj]},
                                inc=("mm", 1) if dt_i == NDT - 1 else None)
                        mm_last = mmv
                        a_tp = K.emit("scalar", (lambda j, pt: lambda g: g.activation(
                            out=wtT[:, :].rearrange("p (dt two) -> p dt two", two=256)[:, :, pt * P:(pt + 1) * P],
                            in_=ps[j][:, :].rearrange("p (dt c) -> p dt c", c=P),
                            func=AF.Copy))(pj, part),
                            waits={"mm": mmv}, inc=("act", 1))
                        ps_rel_act[pj] = a_tp
                    macc_rel[par] = mm_last
                    return a_tp

                def hg_mmscan(cq, a_tp):
                    prev_done = state["hg_done"]
                    mm_hid = 0; mm_gat = 0
                    for part in range(2):
                        for sc in range(4):
                            pj = (0 if part == 0 else 2) + sc // 2
                            mmv = 0
                            for dt_i in range(NDT):
                                w = {"act": max(a_tp, ps_rel_act[pj])} if dt_i == 0 else {"act": a_tp}
                                if dt_i == 0:
                                    w["dve"] = ps_rel_dve[pj]
                                mmv = K.emit("tensor", (lambda pt, s_, j, i: lambda g: g.matmul(
                                    out=ps[j][:, (s_ % 2) * 512:(s_ % 2 + 1) * 512],
                                    lhsT=wtT[:, i * 256 + pt * P:i * 256 + (pt + 1) * P],
                                    rhs=xn[:, i * S + s_ * 512:i * S + (s_ + 1) * 512],
                                    start=(i == 0), stop=(i == NDT - 1)))(part, sc, pj, dt_i),
                                    waits=w, inc=("mm", 1) if dt_i == NDT - 1 else None)
                            if part == 0:
                                mm_hid = mmv
                            else:
                                mm_gat = mmv
                    a_h = 0; a_g = 0
                    for part in range(2):
                        for hf in range(2):
                            pj = (0 if part == 0 else 2) + hf
                            dst = hid_t if part == 0 else gat_t
                            av2 = K.emit("scalar", (lambda j, d3, h2: lambda g: g.activation(
                                out=d3[:, h2 * 1024:(h2 + 1) * 1024], in_=ps[j][:, 0:1024],
                                func=AF.Copy))(pj, dst, hf),
                                waits={"mm": (mm_hid if part == 0 else mm_gat),
                                       "dve": prev_done}, inc=("act", 1))
                            ps_rel_act[pj] = av2
                            if part == 0:
                                a_h = av2
                            else:
                                a_g = av2
                    a_c = K.emit("scalar", lambda g: g.activation(out=ct_t[:, :], in_=gat_t[:, :],
                                                                 func=AF.Sigmoid, scale=-1.0),
                                 waits={"act": a_g, "dve": prev_done}, inc=("act", 1))
                    a_z = 0
                    for hf2 in range(2):
                        a_z = K.emit("scalar", (lambda h3: lambda g: g.activation(
                            out=ps[h3][:, 0:1024], in_=gat_t[:, h3 * 1024:(h3 + 1) * 1024],
                            func=AF.Sigmoid))(hf2), inc=("act", 1))
                    a_s = K.emit("scalar", lambda g: g.activation(out=gat_t[:, :], in_=hid_t[:, :],
                                                                 func=AF.Sigmoid),
                                 waits={"act": a_h}, inc=("act", 1))
                    K.emit("vector", lambda g: g.tensor_scalar(out=hid_t[:, :], in0=hid_t[:, :],
                                                               scalar1=0.5, scalar2=None, op0=ALU.add),
                           waits={"act": a_s}, inc=("dve", 1))
                    K.emit("vector", lambda g: g.tensor_tensor(out=hid_t[:, :], in0=hid_t[:, :],
                                                               in1=gat_t[:, :], op=ALU.max),
                           inc=("dve", 1))
                    dvv2 = 0
                    for hf2 in range(2):
                        dvv2 = K.emit("vector", (lambda h3: lambda g: g.tensor_tensor(
                            out=hid_t[:, h3 * 1024:(h3 + 1) * 1024],
                            in0=hid_t[:, h3 * 1024:(h3 + 1) * 1024],
                            in1=ps[h3][:, 0:1024], op=ALU.mult))(hf2),
                            waits={"act": a_z}, inc=("dve", 1))
                    ps_rel_dve[0] = dvv2; ps_rel_dve[1] = dvv2
                    K.emit("vector", lambda g: g.tensor_tensor_scan(
                        out=gat_t[:, :], data0=ct_t[:, :], data1=hid_t[:, :], initial=0.0,
                        op0=ALU.mult, op1=ALU.add),
                        waits={"act": a_c}, inc=("dve", 1))
                    state["hg_done"] = K.emit("vector", (lambda c2: lambda g: g.tensor_copy(
                        hbf[:, c2 * S:(c2 + 1) * S], gat_t[:, :]))(cq), inc=("dve", 1))

                fma = hg_fma(0)
                a_tp = hg_tp(0, fma)
                for cq in range(NDT):
                    if cq + 1 < NDT:
                        fma_n = hg_fma(cq + 1)
                    hg_mmscan(cq, a_tp)
                    if cq + 1 < NDT:
                        a_tp = hg_tp(cq + 1, fma_n)

                def o_fma(oq):
                    par = oq % 2
                    fma = 0
                    for e in range(E):
                        slq = state["slot"] % 5; state["slot"] += 1
                        wld = K.emit("sync", (lambda ee, ro, q: lambda g: g.dma_start(
                            out=wst[:, q * 1024:(q + 1) * 1024],
                            in_=wout_d[ee, ro:ro + P, :]))(e, oq * P, slq),
                            waits={wst_rel[slq][0]: wst_rel[slq][1]}, inc=(wmsem[slq], 16))
                        tgt = par * 2048
                        if e == 0:
                            fma = K.emit("vector", (lambda q, tg, ee: lambda g: g.tensor_scalar(
                                out=macc[:, tg:tg + 1024], in0=wst[:, q * 1024:(q + 1) * 1024],
                                scalar1=pcols[:, ee:ee + 1], scalar2=None, op0=ALU.mult))(slq, tgt, e),
                                waits={wmsem[slq]: wld, "mm": macc_rel[par]}, inc=("dve", 1))
                        else:
                            fma = K.emit("vector", (lambda q, tg, ee: lambda g: g.scalar_tensor_tensor(
                                out=macc[:, tg:tg + 1024], in0=wst[:, q * 1024:(q + 1) * 1024],
                                scalar=pcols[:, ee:ee + 1], in1=macc[:, tg:tg + 1024],
                                op0=ALU.mult, op1=ALU.add))(slq, tgt, e),
                                waits={wmsem[slq]: wld}, inc=("dve", 1))
                        wst_rel[slq] = ("dve", fma)
                    return fma

                def o_tp(oq, fma):
                    par = oq % 2
                    pj = 2 + par
                    mmv = 0
                    for kt in range(NDT):
                        mmv = K.emit("tensor", (lambda i, j, so: lambda g: g.transpose(
                            out=ps[j][:, i * P:(i + 1) * P], in_=macc[:, so + i * P:so + (i + 1) * P],
                            identity=ident[:, :]))(kt, pj, par * 2048),
                            waits={"dve": fma, "act": ps_rel_act[pj]},
                            inc=("mm", 1) if kt == NDT - 1 else None)
                    macc_rel[par] = mmv
                    a_tp = K.emit("scalar", (lambda j: lambda g: g.activation(
                        out=wotT[:, :], in_=ps[j][:, :], func=AF.Copy))(pj),
                        waits={"mm": mmv}, inc=("act", 1))
                    ps_rel_act[pj] = a_tp
                    return a_tp

                def o_mm(oq, a_tp):
                    for hf in range(2):
                        pj = hf
                        mmv = 0
                        for sc in range(2):
                            for kt in range(NDT):
                                w = {"act": a_tp, "dve": state["hg_done"]}
                                if kt == 0 and sc == 0:
                                    w["dve"] = max(state["hg_done"], ps_rel_dve[pj])
                                mmv = K.emit("tensor", (lambda h2, s_, j, i: lambda g: g.matmul(
                                    out=ps[j][:, s_ * 512:(s_ + 1) * 512],
                                    lhsT=wotT[:, i * P:(i + 1) * P],
                                    rhs=hbf[:, i * S + h2 * 1024 + s_ * 512:
                                            i * S + h2 * 1024 + (s_ + 1) * 512],
                                    start=(i == 0), stop=(i == NDT - 1)))(hf, sc, pj, kt),
                                    waits=w,
                                    inc=("mm", 1) if (kt == NDT - 1 and sc == 1) else None)
                        dvv = K.emit("vector", (lambda o, h2, j: lambda g: g.tensor_tensor(
                            out=mx[:, o * S + h2 * 1024:o * S + (h2 + 1) * 1024],
                            in0=mx[:, o * S + h2 * 1024:o * S + (h2 + 1) * 1024],
                            in1=ps[j][:, 0:1024], op=ALU.add))(oq, hf, pj),
                            waits={"mm": mmv}, inc=("dve", 1))
                        ps_rel_dve[pj] = dvv

                fma = o_fma(0)
                a_tp = o_tp(0, fma)
                for oq in range(NDT):
                    if oq + 1 < NDT:
                        fma_n = o_fma(oq + 1)
                    o_mm(oq, a_tp)
                    if oq + 1 < NDT:
                        a_tp = o_tp(oq + 1, fma_n)
            ev_end = K.c["dve"]
            mix_store = 0
            for dt_i in range(NDT):
                mix_store = K.emit("scalar", (lambda i: lambda g: g.dma_start(
                    out=xfB[i * P:(i + 1) * P, :], in_=mx[:, i * S:(i + 1) * S]))(dt_i),
                    waits={"dve": ev_end}, inc=("xdma", 16))
        K.barrier()

        ffn2_store = ffn_call(xfB, out_d, ("xdma", mix_store), "B", dst_is_tm=True)

        K.emit("sync", lambda g: g.nop(), waits={"xdma": K.c["xdma"], "xt0": K.c["xt0"], "xt1": K.c["xt1"]})

        with nc.Block() as block:
            def replay(name):
                def _(eng):
                    waited = {}
                    for w, inc, fn in K.ops[name]:
                        for s2, v in w.items():
                            if v > waited.get(s2, 0):
                                eng.wait_ge(sems[s2], v)
                                waited[s2] = v
                        r = fn(eng)
                        if inc:
                            r.then_inc(sems[inc[0]], inc[1])
                return _
            block.tensor(replay("tensor"))
            block.vector(replay("vector"))
            block.scalar(replay("scalar"))
            block.gpsimd(replay("gpsimd"))
            block.sync(replay("sync"))

    return nc


_AUX = None


def _aux():
    global _AUX
    if _AUX is None:
        esel = np.zeros((E, E * P), dtype=np.float32)
        for e in range(E):
            esel[e, e * P:(e + 1) * P] = 1.0
        _AUX = {
            "ident": np.eye(P, dtype=np.float32),
            "esel": esel,
            "onesr": np.ones((1, P), dtype=np.float32),
            "onescf": np.ones((P, 1), dtype=np.float32),
            "cvals": np.tile(np.array([[1e-5, 1e-6]], dtype=np.float32), (P, 1)),
            "ecolh": np.broadcast_to((np.arange(E, dtype=np.float32) * CAP - 1)[None, :], (P, E)).copy(),
            "eclimh": np.broadcast_to((np.arange(E, dtype=np.float32) * CAP + CAP - 1)[None, :], (P, E)).copy(),
        }
    return _AUX


_NC_CACHE = None


def kernel(L0_attn_inputs, L0_attn_norm_weight, L0_attn_router_w, L0_attn_router_b,
           L0_attn_w_hg, L0_attn_w_out, L0_ffn_w_router, L0_ffn_w1, L0_ffn_b1,
           L0_ffn_w2, L0_ffn_b2):
    global _NC_CACHE
    if _NC_CACHE is None:
        _NC_CACHE = build_program(debug=False)
    nc = _NC_CACHE
    aux = _aux()
    f32 = lambda a: np.ascontiguousarray(a, dtype=np.float32)
    nw = f32(L0_attn_norm_weight)
    wr = f32(L0_ffn_w_router)
    b1 = f32(L0_ffn_b1)
    b2 = f32(L0_ffn_b2)
    common = {
        "rw": f32(L0_attn_router_w),
        "rb": f32(L0_attn_router_b).reshape(1, E),
        "whg": f32(L0_attn_w_hg),
        "wout": f32(L0_attn_w_out),
        "w1": f32(L0_ffn_w1),
        "w2": f32(L0_ffn_w2),
        "nw8h": np.ascontiguousarray(nw.reshape(NDT, P).T),
        "wr64h": np.ascontiguousarray(wr.reshape(NDT, P, E).transpose(1, 0, 2).reshape(P, NDT * E)),
        "b1ah": np.ascontiguousarray(b1.reshape(E, NJC, P).transpose(2, 0, 1).reshape(P, E * NJC)),
        "b2ah": np.ascontiguousarray(b2.reshape(E, NDT, P).transpose(2, 0, 1).reshape(P, E * NDT)),
        **aux,
    }
    x = f32(L0_attn_inputs)
    in_maps = [dict(common, inp=np.ascontiguousarray(x[b])) for b in range(8)]
    res = run_bass_kernel_spmd(nc, in_maps, core_ids=list(range(8)))
    return np.stack([r["out"] for r in res.results], axis=0)


# version 16 (final)
# speedup vs baseline: 1.1029x; 1.1029x over previous
import numpy as np
import ml_dtypes
import concourse.bass as bass
import concourse.mybir as mybir
from concourse.bass_utils import run_bass_kernel_spmd
from contextlib import ExitStack

F32 = mybir.dt.float32
BF16 = mybir.dt.bfloat16
AF = mybir.ActivationFunctionType
ALU = mybir.AluOpType
AX = mybir.AxisListType.X

P = 128
S = 2048
D = 1024
H = 4096
E = 8
NDT = D // P
NJC = H // P
TC = 1024
CAP = 384
NCH = S // TC
NTS = TC // P

SIM_MODE = False

ENGS = ("tensor", "vector", "scalar", "gpsimd", "sync")
SEMS = ("wd0", "wd1", "wd2", "wd3", "we0", "we1", "wm0", "wm1", "wm2", "wm3", "wm4", "wm5", "wc0", "wc1", "wc2", "wc3", "xt0", "xt1", "xg0", "xg1", "sct", "gt0", "gt1", "xdma", "cdma", "mm", "act", "dve", "pool")
OWN = {"tensor": "mm", "vector": "dve", "scalar": "act", "gpsimd": "pool"}


class Sched:
    def __init__(self):
        self.c = {k: 0 for k in SEMS}
        self.ops = {k: [] for k in ENGS}

    def emit(self, eng, fn, waits=None, inc=None):
        own = OWN.get(eng)
        w = {k: v for k, v in (waits or {}).items() if v > 0}
        if eng == "tensor":
            if SIM_MODE:
                w["mm"] = self.c["mm"]
            else:
                w.pop("mm", None)
        elif own is not None:
            w[own] = self.c[own]
            if inc is None:
                inc = (own, 1)
        self.ops[eng].append((w, inc, fn))
        if inc:
            self.c[inc[0]] += inc[1]
            return self.c[inc[0]]
        return None

    def barrier(self):
        snap = dict(self.c)
        for eng in ("tensor", "vector", "scalar", "gpsimd"):
            w = {s: v for s, v in snap.items() if s != OWN[eng]}
            self.emit(eng, lambda g: g.nop(), waits=w, inc=(OWN[eng], 1))
        snap2 = dict(self.c)
        for eng in ("tensor", "vector", "scalar", "gpsimd", "sync"):
            w = {s: snap2[s] for s in SEMS}
            self.emit(eng, lambda g: g.nop(), waits=w)


def build_program(debug=False):
    nc = bass.Bass("TRN2", target_bir_lowering=False)

    inp = nc.dram_tensor("inp", [S, D], F32, kind="ExternalInput").ap()
    rw_d = nc.dram_tensor("rw", [E, D], F32, kind="ExternalInput").ap()
    rb_d = nc.dram_tensor("rb", [1, E], F32, kind="ExternalInput").ap()
    whg_d = nc.dram_tensor("whg", [E, 2 * D, D], F32, kind="ExternalInput").ap()
    wout_d = nc.dram_tensor("wout", [E, D, D], F32, kind="ExternalInput").ap()
    w1_d = nc.dram_tensor("w1", [E, D, H], F32, kind="ExternalInput").ap()
    w2_d = nc.dram_tensor("w2", [E, H, D], F32, kind="ExternalInput").ap()
    nw8_d = nc.dram_tensor("nw8h", [P, NDT], F32, kind="ExternalInput").ap()
    wr64_d = nc.dram_tensor("wr64h", [P, NDT * E], F32, kind="ExternalInput").ap()
    b1a_d = nc.dram_tensor("b1ah", [P, E * NJC], F32, kind="ExternalInput").ap()
    b2a_d = nc.dram_tensor("b2ah", [P, E * NDT], F32, kind="ExternalInput").ap()
    ident_d = nc.dram_tensor("ident", [P, P], F32, kind="ExternalInput").ap()
    esel_d = nc.dram_tensor("esel", [E, E * P], F32, kind="ExternalInput").ap()
    onesr_d = nc.dram_tensor("onesr", [1, P], F32, kind="ExternalInput").ap()
    onescf_d = nc.dram_tensor("onescf", [P, 1], F32, kind="ExternalInput").ap()
    cvals_d = nc.dram_tensor("cvals", [P, 2], F32, kind="ExternalInput").ap()
    ecol_d = nc.dram_tensor("ecolh", [P, E], F32, kind="ExternalInput").ap()
    eclim_d = nc.dram_tensor("eclimh", [P, E], F32, kind="ExternalInput").ap()
    out_d = nc.dram_tensor("out", [S, D], F32, kind="ExternalOutput").ap()
    xk = "ExternalOutput" if debug else "Internal"
    xfA = nc.dram_tensor("xfA", [D, S], F32, kind=xk).ap()
    xfB = nc.dram_tensor("xfB", [D, S], F32, kind=xk).ap()
    w1c_d = nc.dram_tensor("w1c", [E, D, H], BF16, kind="Internal").ap()
    w2c_d = nc.dram_tensor("w2c", [E, H, D], BF16, kind="Internal").ap()
    slotbuf0 = nc.dram_tensor("slotbuf0", [E * CAP, D], F32, kind="Internal").ap()
    slotbuf1 = nc.dram_tensor("slotbuf1", [E * CAP, D], F32, kind="Internal").ap()
    slotout0 = nc.dram_tensor("slotout0", [E * CAP, D], F32, kind="Internal").ap()
    slotout1 = nc.dram_tensor("slotout1", [E * CAP, D], F32, kind="Internal").ap()

    K = Sched()
    CACHETOT = {}

    with ExitStack() as top:
        def sb(name, shape, dt):
            return top.enter_context(nc.sbuf_tensor(name, shape, dt))

        sems = {s: top.enter_context(nc.semaphore(s)) for s in SEMS}
        ps = [top.enter_context(nc.psum_tensor(f"ps{i}", [P, 1024], F32)) for i in range(4)]

        ident = sb("identS", [P, P], F32)
        esel = sb("eselS", [E, E * P], F32)
        onesr = sb("onesrS", [1, P], F32)
        onescf = sb("onescfS", [P, 1], F32)
        onescb = sb("onescbS", [P, 1], BF16)
        nw8 = sb("nw8", [P, NDT], F32)
        rwraw = sb("rwraw", [E, D], F32)
        rwT = sb("rwT", [P, NDT * E], F32)
        rbrow = sb("rbrow", [1, E], F32)
        wr64 = sb("wr64", [P, NDT * E], F32)
        b1all = sb("b1all", [P, E * NJC], F32)
        b2all = sb("b2all", [P, E * NDT], F32)
        wsumb = sb("wsumb", [P, E], F32)
        probrow = sb("probrow", [1, E], F32)
        pcols = sb("pcols", [P, E], F32)
        r8a = sb("r8a", [1, E], F32)
        r8b = sb("r8b", [1, E], F32)
        tiny = sb("tinyS", [E, 1], F32)
        ecol = sb("ecolS", [P, E], F32)
        eclim = sb("eclimS", [P, E], F32)
        cvals = sb("cvalsS", [P, 2], F32)
        nc.const_aps.register(F32, 1e-5, cvals[:, 0:1])
        nc.const_aps.register(F32, 1e-6, cvals[:, 1:2])

        zscope = ExitStack()
        zbf = zscope.enter_context(nc.sbuf_tensor("zbfS", [P, D], F32))
        for (dst, src) in [(ident[:, :], ident_d[:, :]), (esel[:, :], esel_d[:, :]),
                           (onesr[:, :], onesr_d[:, :]), (onescf[:, :], onescf_d[:, :]),
                           (cvals[:, :], cvals_d[:, :]),
                           (rwraw[:, :], rw_d[:, :]),
                           (rbrow[:, :], rb_d[:, :]), (nw8[:, :], nw8_d[:, :]),
                           (wr64[:, :], wr64_d[:, :]), (b1all[:, :], b1a_d[:, :]),
                           (b2all[:, :], b2a_d[:, :]),
                           (ecol[:, :], ecol_d[:, :]), (eclim[:, :], eclim_d[:, :])]:
            K.emit("gpsimd", (lambda d, s: lambda g: g.dma_start(out=d, in_=s))(dst, src),
                   inc=("cdma", 16))
        cload = K.c["cdma"]
        dz = K.emit("vector", lambda g: g.memset(zbf[:, :], 0.0), inc=("dve", 1))
        for sbd in (slotbuf0, slotbuf1):
            for k in range(E * CAP // P):
                K.emit("gpsimd", (lambda d2, kk: lambda g: g.dma_start(
                    out=d2[kk * P:(kk + 1) * P, :], in_=zbf[:, :]))(sbd, k),
                    waits={"dve": dz}, inc=("cdma", 16))
        cload = K.c["cdma"]
        zscope.close()
        K.emit("vector", lambda g: g.tensor_copy(onescb[:, :], onescf[:, :]),
               waits={"cdma": cload}, inc=("dve", 1))
        K.barrier()

        for dt_i in range(NDT):
            mv = K.emit("tensor", (lambda i: lambda g: g.transpose(
                out=ps[i % 2][0:P, 0:8], in_=rwraw[0:E, i * P:(i + 1) * P],
                identity=ident[0:E, 0:E]))(dt_i),
                waits={"cdma": cload, "dve": K.c["dve"]}, inc=("mm", 1))
            K.emit("vector", (lambda i: lambda g: g.tensor_copy(
                rwT[:, i * 8:(i + 1) * 8], ps[i % 2][0:P, 0:8]))(dt_i),
                waits={"mm": mv}, inc=("dve", 1))
        for dt_i in range(NDT):
            mv = K.emit("tensor", (lambda i: lambda g: g.matmul(
                out=ps[2][0:E, 0:1], lhsT=wr64[:, i * 8:(i + 1) * 8], rhs=onescf[:, :],
                start=(i == 0), stop=(i == NDT - 1)))(dt_i),
                waits={"cdma": cload}, inc=("mm", 1) if dt_i == NDT - 1 else None)
        dv = K.emit("vector", lambda g: g.tensor_copy(tiny[:, :], ps[2][0:E, 0:1]),
                    waits={"mm": mv}, inc=("dve", 1))
        mv = K.emit("tensor", lambda g: g.transpose(out=ps[3][0:1, 0:E], in_=tiny[:, :],
                                                    identity=ident[0:E, 0:E]),
                    waits={"dve": dv}, inc=("mm", 1))
        dv = K.emit("vector", lambda g: g.tensor_copy(r8a[:, :], ps[3][0:1, 0:E]),
                    waits={"mm": mv}, inc=("dve", 1))
        dv = K.emit("vector", lambda g: g.tensor_scalar(
            out=r8a[:, :], in0=r8a[:, :], scalar1=-1.0, scalar2=None, op0=ALU.mult),
            inc=("dve", 1))
        mv = K.emit("tensor", lambda g: g.matmul(out=ps[3][0:P, 0:E], lhsT=onesr[:, :],
                                                 rhs=r8a[:, :], start=True, stop=True),
                    waits={"dve": dv}, inc=("mm", 1))
        K.emit("vector", lambda g: g.tensor_copy(wsumb[:, :], ps[3][0:P, 0:E]),
               waits={"mm": mv}, inc=("dve", 1))
        K.barrier()

        def ffn_call(src_xf, dst_xf, src_ready, tag, src_is_tm=False, dst_is_tm=False):
            slotb = (slotbuf0, slotbuf1)
            sloto = (slotout0, slotout1)
            with ExitStack() as fs:
                def fsb(name, shape, dt):
                    return fs.enter_context(nc.sbuf_tensor(name + tag, shape, dt))
                xc = fsb("xc", [P, 2 * NDT * TC], F32)
                u1tm = fsb("u1tm", [P, NTS * D], F32)
                hsb = fsb("hsb", [P, NJC * CAP], BF16)
                w1b = fsb("w1b", [P, 4 * 2048], BF16)
                w2b = fsb("w2b", [P, 2 * 1024], BF16)
                xg = fsb("xg", [P, 2 * D], F32)
                xg_f = fsb("xg_f", [P, NDT * CAP], BF16)
                outf = fsb("outf", [P, NDT * CAP], F32)
                tmst = fsb("tmst", [P, 2 * D], F32)
                tmpu = fsb("tmpu", [P, D], F32)
                raw_tm = fsb("raw_tm", [P, NTS * E], F32)
                lg_tm = fsb("lg_tm", [P, NTS * E], F32)
                t1_tm = fsb("t1_tm", [P, NTS * E], F32)
                eq_tm = fsb("eq_tm", [P, NTS * E], F32)
                msk_tm = fsb("msk_tm", [P, NTS * E], F32)
                ge_tm = fsb("ge_tm", [P, NTS * E], F32)
                ex_tm = fsb("ex_tm", [P, NTS * E], F32)
                wg_tm = fsb("wg_tm", [P, NTS * E], F32)
                pfx_tm = fsb("pfx_tm", [P, NTS * E], F32)
                sid_tm = fsb("sid_tm", [P, NTS * E], F32)
                c8a = fsb("c8a", [P, NTS], F32)
                c8b = fsb("c8b", [P, NTS], F32)
                c8c = fsb("c8c", [P, NTS], F32)
                c8d = fsb("c8d", [P, NTS], F32)
                c8e = fsb("c8e", [P, NTS], F32)
                m_tm = fsb("m_tm", [P, NTS], F32)
                nr_tm = fsb("nr_tm", [P, NTS], F32)
                wg1c = fsb("wg1c", [P, 2 * NTS], F32)
                wg2c = fsb("wg2c", [P, 2 * NTS], F32)
                idx1u = fsb("idx1u", [P, 2 * NTS], mybir.dt.uint32)
                idx2u = fsb("idx2u", [P, 2 * NTS], mybir.dt.uint32)
                maskT = fsb("maskT", [E, TC], F32)
                ones8 = fsb("ones8", [E, TC], F32)
                pfxT = fsb("pfxT", [E, TC], F32)

                w1st_rel = [0, 0, 0, 0]; w1b_rel = [0, 0, 0, 0]
                w2st_rel = [0, 0]; w2b_rel = [0, 0]
                wdsem = ["wd0", "wd1", "wd2", "wd3"]; wesem = ["we0", "we1"]
                xgsem = ["xg0", "xg1"]
                xg_rel = [0, 0]
                gb_rel = [("dve", 0), ("dve", 0)]
                ps_b0 = [("dve", 0)] * 4
                b1_rel = [("dve", 0), ("dve", 0)]
                store_ms = src_ready
                xtsem = ["xt0", "xt1"]
                tmst_rel = [("dve", 0), ("dve", 0)]

                K.emit("vector", lambda g: g.memset(ones8[:, :], 1.0), inc=("dve", 1))

                def addw(w, sem, val):
                    if sem == "dve":
                        w["dve"] = max(w.get("dve", 0), val)
                    else:
                        w[sem] = max(w.get(sem, 0), val)

                def tp_assemble(srcs, dst_aps, n_per, tp_waits=None, copy_waits=None,
                                add_to=None, ts_col=None, op_rows=P, in_rows=P):
                    nh = len(srcs) // 2
                    mm_last = [0, 0]
                    for i, in_ap in enumerate(srcs):
                        j2 = i // nh
                        w = dict(tp_waits or {})
                        addw(w, b1_rel[j2][0], b1_rel[j2][1])
                        mm_last[j2] = K.emit("tensor", (lambda a, k, j3: lambda g: g.transpose(
                            out=ps[2 + j3][0:op_rows, 512 + k * n_per:512 + (k + 1) * n_per],
                            in_=a, identity=ident[0:in_rows, 0:in_rows]))(in_ap, i % nh, j2),
                            waits=w, inc=("mm", 1))
                    last = 0
                    for j2 in range(2):
                        w = dict(copy_waits or {})
                        w["mm"] = max(w.get("mm", 0), mm_last[j2])
                        if add_to is not None:
                            dst = add_to[:, nh * j2:nh * (j2 + 1), ts_col * P:(ts_col + 1) * P]
                            last = K.emit("vector", (lambda j3, d4: lambda g: g.tensor_tensor(
                                out=d4, in0=d4,
                                in1=ps[2 + j3][:, 512:512 + 4 * P].rearrange(
                                    "p (k c) -> p k c", c=P),
                                op=ALU.add))(j2, dst),
                                waits=w, inc=("dve", 1))
                        else:
                            last = K.emit("vector", (lambda j3, d4, np2: lambda g: g.tensor_copy(
                                d4, ps[2 + j3][0:op_rows, 512:512 + nh * np2].rearrange(
                                    "p (k c) -> p k c", c=np2)))(j2, dst_aps[j2], n_per),
                                waits=w, inc=("dve", 1))
                        b1_rel[j2] = ("dve", last)
                    return last

                stref = [src_ready]
                cache_ms = {}
                cache_ring = [("wc0", 0), ("wc1", 0), ("wc2", 0), ("wc3", 0)]
                cache_i = [0]

                def emit_cache(e):
                    if e >= E:
                        return
                    ms = []
                    for (srcw, dstw) in ((w1_d, w1c_d), (w2_d, w2c_d)):
                        ri = cache_i[0] % 4
                        sem_p, val_p = cache_ring[ri]
                        v = K.emit("gpsimd", (lambda s2, d2, ee: lambda g: g.dma_start(
                            out=d2[ee], in_=s2[ee]))(srcw, dstw, e),
                            waits={sem_p: val_p}, inc=(sem_p, 16))
                        cache_ring[ri] = (sem_p, v)
                        ms.append((sem_p, v))
                        cache_i[0] += 1
                    cache_ms[e] = ms

                def chunk_head(ch):
                    t0 = ch * TC
                    xco = ch * NDT * TC
                    ixo = ch * NTS
                    sbuf_d = slotb[ch]
                    store_ms = stref[0]
                    sct_prev = K.c["sct"]
                    gt_prev0 = K.c["gt0"]; gt_prev1 = K.c["gt1"]
                    if not src_is_tm:
                        xld = 0
                        for dt_i in range(NDT):
                            xld = K.emit("scalar", (lambda i, tt: lambda g: g.dma_start(
                                out=xc[:, xco + i * TC:xco + (i + 1) * TC],
                                in_=src_xf[i * P:(i + 1) * P, tt:tt + TC]))(dt_i, t0),
                                waits={store_ms[0]: store_ms[1]}, inc=("xdma", 16))
                    u1_done = 0
                    xcv = xc[:, xco:xco + NDT * TC].rearrange("p (dt t) -> p dt t", t=TC)
                    for st in range(NTS):
                        sl = st % 2
                        if src_is_tm:
                            ld = K.emit("scalar", (lambda s_, q, tt: lambda g: g.dma_start(
                                out=tmst[:, q * D:(q + 1) * D],
                                in_=src_xf[tt + s_ * P:tt + (s_ + 1) * P, :]))(st, sl, t0),
                                waits={store_ms[0]: store_ms[1],
                                       tmst_rel[sl][0]: tmst_rel[sl][1]},
                                inc=(xtsem[sl], 16))
                            tpw = {xtsem[sl]: ld}
                            cw = {store_ms[0]: store_ms[1]} if store_ms[1] else {}
                            dv_fill = tp_assemble(
                                [tmst[:, sl * D + i * P:sl * D + (i + 1) * P] for i in range(NDT)],
                                [xcv[:, 0:4, st * P:(st + 1) * P],
                                 xcv[:, 4:8, st * P:(st + 1) * P]],
                                P, tpw, cw)
                            tmst_rel[sl] = ("mm", K.c["mm"])
                            rd_w = {xtsem[sl]: ld}
                        else:
                            dv_fill = tp_assemble(
                                [xc[:, xco + i * TC + st * P:xco + i * TC + (st + 1) * P] for i in range(NDT)],
                                [tmst[:, sl * D:sl * D + 4 * P],
                                 tmst[:, sl * D + 4 * P:sl * D + 8 * P]],
                                P, {"xdma": xld},
                                {tmst_rel[sl][0]: tmst_rel[sl][1]} if tmst_rel[sl][1] else {})
                            tmst_rel[sl] = ("dve", dv_fill)
                            rd_w = {"dve": dv_fill}
                        K.emit("vector", (lambda q, s_: lambda g: g.reduce_sum(
                            c8a[:, s_:s_ + 1], tmst[:, q * D:(q + 1) * D], axis=AX))(sl, st),
                            waits=rd_w, inc=("dve", 1))
                        avq = K.emit("scalar", (lambda q: lambda g: g.activation(
                            out=tmpu[:, 0:D], in_=tmst[:, q * D:(q + 1) * D],
                            func=AF.Square))(sl),
                            waits=rd_w, inc=("act", 1))
                        K.emit("vector", (lambda s_: lambda g: g.reduce_sum(
                            c8b[:, s_:s_ + 1], tmpu[:, 0:D], axis=AX))(st),
                            waits={"act": avq}, inc=("dve", 1))
                        K.emit("vector", (lambda s_: lambda g: g.tensor_scalar(
                            out=m_tm[:, s_:s_ + 1], in0=c8a[:, s_:s_ + 1], scalar1=1.0 / D,
                            scalar2=None, op0=ALU.mult))(st), inc=("dve", 1))
                        K.emit("vector", (lambda s_: lambda g: g.tensor_scalar(
                            out=c8c[:, s_:s_ + 1], in0=c8b[:, s_:s_ + 1], scalar1=1.0 / D,
                            scalar2=None, op0=ALU.mult))(st), inc=("dve", 1))
                        K.emit("vector", (lambda s_: lambda g: g.tensor_tensor(
                            out=c8d[:, s_:s_ + 1], in0=m_tm[:, s_:s_ + 1],
                            in1=m_tm[:, s_:s_ + 1], op=ALU.mult))(st), inc=("dve", 1))
                        dv = K.emit("vector", (lambda s_: lambda g: g.tensor_tensor(
                            out=c8c[:, s_:s_ + 1], in0=c8c[:, s_:s_ + 1],
                            in1=c8d[:, s_:s_ + 1], op=ALU.subtract))(st), inc=("dve", 1))
                        av = K.emit("scalar", (lambda s_: lambda g: g.activation(
                            out=c8d[:, s_:s_ + 1], in_=c8c[:, s_:s_ + 1], func=AF.Sqrt,
                            bias=1e-5))(st), waits={"dve": dv}, inc=("act", 1))
                        dv = K.emit("vector", (lambda s_: lambda g: g.reciprocal(
                            nr_tm[:, s_:s_ + 1], c8d[:, s_:s_ + 1]))(st),
                            waits={"act": av}, inc=("dve", 1))
                        u1_done = K.emit("vector", (lambda q, s_: lambda g: g.tensor_scalar(
                            out=u1tm[:, s_ * D:(s_ + 1) * D], in0=tmst[:, q * D:(q + 1) * D],
                            scalar1=m_tm[:, s_:s_ + 1], scalar2=nr_tm[:, s_:s_ + 1],
                            op0=ALU.subtract, op1=ALU.mult))(sl, st),
                            waits={"sct": sct_prev}, inc=("dve", 1))
                        if not src_is_tm:
                            tmst_rel[sl] = ("dve", u1_done)
                    for ts_i in range(NTS):
                        pj = 2 + ts_i % 2
                        mmv = 0
                        for dt_i in range(NDT):
                            w = {}
                            addw(w, ps_b0[pj][0], ps_b0[pj][1])
                            mmv = K.emit("tensor", (lambda t, i, j: lambda g: g.matmul(
                                out=ps[j][:, 0:8], lhsT=xc[:, xco + i * TC + t * P:xco + i * TC + (t + 1) * P],
                                rhs=wr64[:, i * 8:(i + 1) * 8],
                                start=(i == 0), stop=(i == NDT - 1)))(ts_i, dt_i, pj),
                                waits=w if dt_i == 0 else None,
                                inc=("mm", 1) if dt_i == NDT - 1 else None)
                        dvv = K.emit("vector", (lambda t, j: lambda g: g.tensor_copy(
                            raw_tm[:, t * 8:(t + 1) * 8], ps[j][:, 0:8]))(ts_i, pj),
                            waits={"mm": mmv}, inc=("dve", 1))
                        ps_b0[pj] = ("dve", dvv)
                    def stage(fn_maker, eng="vector", waits=None, incsem="dve"):
                        last = 0
                        for t in range(NTS):
                            last = K.emit(eng, fn_maker(t), waits=waits, inc=(incsem, 1))
                        return last
                    stage(lambda t: lambda g: g.scalar_tensor_tensor(
                        out=t1_tm[:, t * 8:(t + 1) * 8], in0=wsumb[:, :], scalar=m_tm[:, t:t + 1],
                        in1=raw_tm[:, t * 8:(t + 1) * 8], op0=ALU.mult, op1=ALU.add))
                    stage(lambda t: lambda g: g.tensor_scalar(
                        out=lg_tm[:, t * 8:(t + 1) * 8], in0=t1_tm[:, t * 8:(t + 1) * 8],
                        scalar1=nr_tm[:, t:t + 1], scalar2=None, op0=ALU.mult))
                    stage(lambda t: lambda g: g.reduce_max(
                        c8a[:, t:t + 1], lg_tm[:, t * 8:(t + 1) * 8], axis=AX))
                    stage(lambda t: lambda g: g.tensor_scalar(
                        out=c8b[:, t:t + 1], in0=c8a[:, t:t + 1], scalar1=-1.0, scalar2=None,
                        op0=ALU.mult))
                    stage(lambda t: lambda g: g.tensor_scalar(
                        out=eq_tm[:, t * 8:(t + 1) * 8], in0=lg_tm[:, t * 8:(t + 1) * 8],
                        scalar1=c8a[:, t:t + 1], scalar2=None, op0=ALU.is_equal))
                    stage(lambda t: lambda g: g.scalar_tensor_tensor(
                        out=msk_tm[:, t * 8:(t + 1) * 8], in0=eq_tm[:, t * 8:(t + 1) * 8],
                        scalar=-1e30, in1=lg_tm[:, t * 8:(t + 1) * 8], op0=ALU.mult, op1=ALU.add))
                    stage(lambda t: lambda g: g.reduce_max(
                        c8c[:, t:t + 1], msk_tm[:, t * 8:(t + 1) * 8], axis=AX))
                    stage(lambda t: lambda g: g.tensor_scalar(
                        out=ge_tm[:, t * 8:(t + 1) * 8], in0=lg_tm[:, t * 8:(t + 1) * 8],
                        scalar1=c8c[:, t:t + 1], scalar2=None, op0=ALU.is_ge))
                    d_lg = K.c["dve"]
                    a_ex = stage(lambda t: lambda g: g.activation(
                        out=ex_tm[:, t * 8:(t + 1) * 8], in_=lg_tm[:, t * 8:(t + 1) * 8],
                        func=AF.Exp, bias=c8b[:, t:t + 1]), eng="scalar",
                        waits={"dve": d_lg}, incsem="act")
                    stage(lambda t: lambda g: g.reduce_sum(
                        c8d[:, t:t + 1], ex_tm[:, t * 8:(t + 1) * 8], axis=AX),
                        waits={"act": a_ex})
                    stage(lambda t: lambda g: g.reciprocal(c8e[:, t:t + 1], c8d[:, t:t + 1]))
                    stage(lambda t: lambda g: g.tensor_scalar(
                        out=wg_tm[:, t * 8:(t + 1) * 8], in0=ex_tm[:, t * 8:(t + 1) * 8],
                        scalar1=c8e[:, t:t + 1], scalar2=0.5, op0=ALU.mult, op1=ALU.mult))
                    d_wg = stage(lambda t: lambda g: g.tensor_tensor(
                        out=wg_tm[:, t * 8:(t + 1) * 8], in0=wg_tm[:, t * 8:(t + 1) * 8],
                        in1=ge_tm[:, t * 8:(t + 1) * 8], op=ALU.mult))
                    tp_assemble([ge_tm[:, t * 8:(t + 1) * 8] for t in range(NTS)],
                                [maskT[0:E, 0:4 * P], maskT[0:E, 4 * P:8 * P]],
                                P, {"dve": d_wg}, op_rows=E)
                    d_mT = K.c["dve"]
                    d_pT = K.emit("vector", lambda g: g.tensor_tensor_scan(
                        out=pfxT[0:E, :], data0=ones8[0:E, :], data1=maskT[0:E, :],
                        initial=0.0, op0=ALU.mult, op1=ALU.add),
                        waits={"dve": d_mT}, inc=("dve", 1))
                    pfv = pfx_tm[:, :].rearrange("p (k c) -> p k c", c=8)
                    tp_assemble([pfxT[0:E, t * P:(t + 1) * P] for t in range(NTS)],
                                [pfv[:, 0:4, :], pfv[:, 4:8, :]],
                                8, {"dve": d_pT}, in_rows=E)
                    d_pfx = K.c["dve"]
                    stage(lambda t: lambda g: g.tensor_tensor(
                        out=sid_tm[:, t * 8:(t + 1) * 8], in0=pfx_tm[:, t * 8:(t + 1) * 8],
                        in1=ecol[:, :], op=ALU.add), waits={"dve": d_pfx})
                    stage(lambda t: lambda g: g.tensor_tensor(
                        out=sid_tm[:, t * 8:(t + 1) * 8], in0=sid_tm[:, t * 8:(t + 1) * 8],
                        in1=eclim[:, :], op=ALU.min))
                    stage(lambda t: lambda g: g.tensor_tensor(
                        out=msk_tm[:, t * 8:(t + 1) * 8], in0=ge_tm[:, t * 8:(t + 1) * 8],
                        in1=eq_tm[:, t * 8:(t + 1) * 8], op=ALU.subtract))
                    stage(lambda t: lambda g: g.tensor_tensor(
                        out=t1_tm[:, t * 8:(t + 1) * 8], in0=eq_tm[:, t * 8:(t + 1) * 8],
                        in1=sid_tm[:, t * 8:(t + 1) * 8], op=ALU.mult))
                    stage(lambda t: lambda g: g.reduce_sum(
                        c8a[:, t:t + 1], t1_tm[:, t * 8:(t + 1) * 8], axis=AX))
                    stage(lambda t: lambda g: g.tensor_tensor(
                        out=t1_tm[:, t * 8:(t + 1) * 8], in0=msk_tm[:, t * 8:(t + 1) * 8],
                        in1=sid_tm[:, t * 8:(t + 1) * 8], op=ALU.mult))
                    stage(lambda t: lambda g: g.reduce_sum(
                        c8b[:, t:t + 1], t1_tm[:, t * 8:(t + 1) * 8], axis=AX))
                    stage(lambda t: lambda g: g.tensor_tensor(
                        out=t1_tm[:, t * 8:(t + 1) * 8], in0=eq_tm[:, t * 8:(t + 1) * 8],
                        in1=wg_tm[:, t * 8:(t + 1) * 8], op=ALU.mult))
                    stage(lambda t: lambda g: g.reduce_sum(
                        wg1c[:, ixo + t:ixo + t + 1], t1_tm[:, t * 8:(t + 1) * 8], axis=AX))
                    stage(lambda t: lambda g: g.tensor_tensor(
                        out=t1_tm[:, t * 8:(t + 1) * 8], in0=msk_tm[:, t * 8:(t + 1) * 8],
                        in1=wg_tm[:, t * 8:(t + 1) * 8], op=ALU.mult))
                    stage(lambda t: lambda g: g.reduce_sum(
                        wg2c[:, ixo + t:ixo + t + 1], t1_tm[:, t * 8:(t + 1) * 8], axis=AX))
                    stage(lambda t: lambda g: g.tensor_copy(
                        idx1u[:, ixo + t:ixo + t + 1], c8a[:, t:t + 1]),
                        waits={"gt0": gt_prev0, "gt1": gt_prev1})
                    d_idx = stage(lambda t: lambda g: g.tensor_copy(
                        idx2u[:, ixo + t:ixo + t + 1], c8b[:, t:t + 1]))
                    for st in range(NTS):
                        for idxt in (idx1u, idx2u):
                            K.emit("gpsimd", (lambda s_, ix: lambda g: g.indirect_dma_start(
                                out=sbuf_d[:, :], out_offset=bass.IndirectOffsetOnAxis(
                                    ap=ix[:, ixo + s_:ixo + s_ + 1], axis=0),
                                in_=u1tm[:, s_ * D:(s_ + 1) * D], in_offset=None,
                                oob_is_err=False))(st, idxt),
                                waits={"dve": d_idx}, inc=("sct", 16))
                    return K.c["sct"]
                def chunk_body(ch, sct_all):
                    first_chunk_cache = (tag == "A" and ch == 0)
                    t0 = ch * TC
                    xco = ch * NDT * TC
                    ixo = ch * NTS
                    sbuf_d = slotb[ch]
                    sbuf_o = sloto[ch]
                    xcv = xc[:, xco:xco + NDT * TC].rearrange("p (dt t) -> p dt t", t=TC)
                    store_ms = stref[0]
                    gelu_all_prev = 0
                    for e in range(E):
                        if first_chunk_cache:
                            emit_cache(e + 2)
                            cw1 = {cache_ms[e][0][0]: cache_ms[e][0][1]}
                            cw2 = {cache_ms[e][1][0]: cache_ms[e][1][1]}
                        else:
                            cw1 = dict(CACHETOT)
                            cw2 = dict(CACHETOT)
                        xg_done = 0
                        xgfv = xg_f[:, :].rearrange("p (dt c) -> p dt c", c=CAP)
                        for k in range(CAP // P):
                            ksl = k % 2
                            gld = K.emit("sync", (lambda ee, kk, q: lambda g: g.dma_start(
                                out=xg[:, q * D:(q + 1) * D],
                                in_=sbuf_d[ee * CAP + kk * P:ee * CAP + (kk + 1) * P, :]))(e, k, ksl),
                                waits={"sct": sct_all, "mm": xg_rel[ksl]},
                                inc=(xgsem[ksl], 16))
                            xg_done = tp_assemble(
                                [xg[:, ksl * D + i * P:ksl * D + (i + 1) * P] for i in range(NDT)],
                                [xgfv[:, 0:4, k * P:(k + 1) * P],
                                 xgfv[:, 4:8, k * P:(k + 1) * P]],
                                P, {xgsem[ksl]: gld}, {"mm": K.c["mm"]})
                            xg_rel[ksl] = K.c["mm"]
                        gelu_ms = {}
                        w1cast = 0
                        for jc in range(NJC):
                            if jc % 2 == 0:
                                pr = (jc // 2) % 4
                                w1cast = K.emit("sync", (lambda ee, j, q: lambda g: g.dma_start(
                                    out=w1b[:, q * 2048:(q + 1) * 2048].rearrange(
                                        "p (dt j) -> p dt j", j=2 * P),
                                    in_=w1c_d[ee].rearrange("(dt p) h -> p dt h", p=P)[:, :, j * P:(j + 2) * P]))(e, jc, pr),
                                    waits=dict(cw1, mm=w1b_rel[pr]), inc=(wdsem[pr], 16))
                                w1st_rel[pr] = w1cast
                            pr = (jc // 2) % 4
                            joff = jc % 2
                            hpr = jc % 2
                            mmv = 0
                            for dt_i in range(NDT):
                                w = {"dve": xg_done, wdsem[pr]: w1cast}
                                if dt_i == 0:
                                    w["act"] = gelu_ms.get(jc - 2, gelu_all_prev)
                                    addw(w, ps_b0[hpr][0], ps_b0[hpr][1])
                                mmv = K.emit("tensor", (lambda q, jo, i, hp: lambda g: g.matmul(
                                    out=ps[hp][:, 0:CAP],
                                    lhsT=w1b[:, q * 2048 + i * 256 + jo * P:
                                             q * 2048 + i * 256 + jo * P + P],
                                    rhs=xg_f[:, i * CAP:(i + 1) * CAP],
                                    start=(i == 0), stop=(i == NDT - 1)))(pr, joff, dt_i, hpr),
                                    waits=w, inc=("mm", 1) if dt_i == NDT - 1 else None)
                            w1b_rel[pr] = mmv
                            gelu_ms[jc] = K.emit("scalar", (lambda ee, j, hp: lambda g: g.activation(
                                out=hsb[:, j * CAP:(j + 1) * CAP], in_=ps[hp][:, 0:CAP],
                                func=AF.Gelu_apprx_tanh,
                                bias=b1all[:, ee * NJC + j:ee * NJC + j + 1]))(e, jc, hpr),
                                waits={"mm": mmv}, inc=("act", 1))
                            ps_b0[hpr] = ("act", gelu_ms[jc])
                        gelu_all = gelu_ms[NJC - 1]
                        gelu_all_prev = gelu_all
                        w2cast = 0
                        ev_ms = 0
                        for half in range(2):
                            for jc in range(NJC):
                                if jc % 2 == 0:
                                    pr = (jc // 2) % 2
                                    w2cast = K.emit("sync", (lambda ee, j, hf, q: lambda g: g.dma_start(
                                        out=w2b[:, q * 1024:(q + 1) * 1024].rearrange(
                                            "p (jj o) -> p jj o", o=512),
                                        in_=w2c_d[ee].rearrange("(jj p) o -> p jj o", p=P)[:, j:j + 2, hf * 512:(hf + 1) * 512]))(e, jc, half, pr),
                                        waits=dict(cw2, mm=w2b_rel[pr]), inc=(wesem[pr], 16))
                                    w2st_rel[pr] = w2cast
                                pr = (jc // 2) % 2
                                joff = jc % 2
                                mmv = 0
                                for oc_rel in range(4):
                                    w = {"act": gelu_ms[jc], wesem[pr]: w2cast}
                                    if jc == 0:
                                        w["act"] = max(w["act"], gelu_all)
                                        addw(w, ps_b0[oc_rel][0], ps_b0[oc_rel][1])
                                    mmv = K.emit("tensor", (lambda j, q, jo, o: lambda g: g.matmul(
                                        out=ps[o][:, 0:CAP],
                                        lhsT=w2b[:, q * 1024 + jo * 512 + o * P:
                                                 q * 1024 + jo * 512 + o * P + P],
                                        rhs=hsb[:, j * CAP:(j + 1) * CAP],
                                        start=(j == 0), stop=(j == NJC - 1)))(jc, pr, joff, oc_rel),
                                        waits=w, inc=("mm", 1) if oc_rel == 3 else None)
                                w2b_rel[pr] = mmv
                            m_p2 = mmv
                            for oc_rel in range(4):
                                oc = half * 4 + oc_rel
                                ev_ms = K.emit("vector", (lambda ee, o, orl: lambda g: g.tensor_scalar(
                                    out=outf[:, o * CAP:(o + 1) * CAP], in0=ps[orl][:, 0:CAP],
                                    scalar1=b2all[:, ee * NDT + o:ee * NDT + o + 1],
                                    scalar2=None, op0=ALU.add))(e, oc, oc_rel),
                                    waits={"mm": m_p2}, inc=("dve", 1))
                                ps_b0[oc_rel] = ("dve", ev_ms)
                        for k in range(CAP // P):
                            sl = k % 2
                            cpw = {tmst_rel[sl][0]: tmst_rel[sl][1]} if tmst_rel[sl][1] else {}
                            cpv = tp_assemble(
                                [outf[:, i * CAP + k * P:i * CAP + (k + 1) * P] for i in range(NDT)],
                                [tmst[:, sl * D:sl * D + 4 * P],
                                 tmst[:, sl * D + 4 * P:sl * D + 8 * P]],
                                P, {"dve": ev_ms}, cpw)
                            K.emit("scalar", (lambda ee, kk, q: lambda g: g.dma_start(
                                out=sbuf_o[ee * CAP + kk * P:ee * CAP + (kk + 1) * P, :],
                                in_=tmst[:, q * D:(q + 1) * D]))(e, k, sl),
                                waits={"dve": cpv}, inc=(xtsem[sl], 16))
                            tmst_rel[sl] = (xtsem[sl], K.c[xtsem[sl]])
                    sout0 = K.c["xt0"]; sout1 = K.c["xt1"]
                    for ts_i in range(NTS):
                        w = {"xt0": sout0, "xt1": sout1}
                        addw(w, gb_rel[0][0], gb_rel[0][1])
                        addw(w, tmst_rel[0][0], tmst_rel[0][1])
                        g1 = K.emit("gpsimd", (lambda t: lambda g: g.indirect_dma_start(
                            out=tmst[:, 0:D], out_offset=None, in_=sbuf_o[:, :],
                            in_offset=bass.IndirectOffsetOnAxis(ap=idx1u[:, ixo + t:ixo + t + 1], axis=0),
                            oob_is_err=False))(ts_i),
                            waits=w, inc=("gt0", 16))
                        w = {"xt0": sout0, "xt1": sout1}
                        addw(w, gb_rel[1][0], gb_rel[1][1])
                        addw(w, tmst_rel[1][0], tmst_rel[1][1])
                        g2 = K.emit("gpsimd", (lambda t: lambda g: g.indirect_dma_start(
                            out=tmst[:, D:2 * D], out_offset=None, in_=sbuf_o[:, :],
                            in_offset=bass.IndirectOffsetOnAxis(ap=idx2u[:, ixo + t:ixo + t + 1], axis=0),
                            oob_is_err=False))(ts_i),
                            waits=w, inc=("gt1", 16))
                        d_t1 = K.emit("vector", (lambda t: lambda g: g.tensor_scalar(
                            out=tmpu[:, 0:D], in0=tmst[:, 0:D], scalar1=wg1c[:, ixo + t:ixo + t + 1],
                            scalar2=None, op0=ALU.mult))(ts_i),
                            waits={"gt0": g1}, inc=("dve", 1))
                        gb_rel[0] = ("dve", d_t1)
                        tmst_rel[0] = ("dve", d_t1)
                        d_cb = K.emit("vector", (lambda t: lambda g: g.scalar_tensor_tensor(
                            out=tmst[:, D:2 * D], in0=tmst[:, D:2 * D], scalar=wg2c[:, ixo + t:ixo + t + 1],
                            in1=tmpu[:, 0:D], op0=ALU.mult, op1=ALU.add))(ts_i),
                            waits={"gt1": g2}, inc=("dve", 1))
                        cb_last = tp_assemble(
                            [tmst[:, D + i * P:D + (i + 1) * P] for i in range(NDT)],
                            [None, None], P, {"dve": d_cb}, None, add_to=xcv, ts_col=ts_i)

                        gb_rel[1] = ("dve", cb_last)
                        tmst_rel[1] = ("dve", cb_last)
                    ev_end = K.c["dve"]
                    if dst_is_tm:
                        for st in range(NTS):
                            sl = st % 2
                            mm_last2 = 0
                            for dt_i in range(NDT):
                                w = {"dve": ev_end}
                                if dt_i == 0:
                                    addw(w, ps_b0[sl][0], ps_b0[sl][1])
                                mm_last2 = K.emit("tensor", (lambda q, i, s_: lambda g: g.transpose(
                                    out=ps[q][:, i * P:(i + 1) * P],
                                    in_=xc[:, xco + i * TC + s_ * P:xco + i * TC + (s_ + 1) * P],
                                    identity=ident[:, :]))(sl, dt_i, st),
                                    waits=w, inc=("mm", 1) if dt_i == NDT - 1 else None)
                            cpv = K.emit("vector", (lambda q: lambda g: g.tensor_copy(
                                tmst[:, q * D:(q + 1) * D], ps[q][:, :]))(sl),
                                waits={"mm": mm_last2, tmst_rel[sl][0]: tmst_rel[sl][1]},
                                inc=("dve", 1))
                            ps_b0[sl] = ("dve", cpv)
                            K.emit("scalar", (lambda q, s_, tt: lambda g: g.dma_start(
                                out=dst_xf[tt + s_ * P:tt + (s_ + 1) * P, :],
                                in_=tmst[:, q * D:(q + 1) * D]))(sl, st, t0),
                                waits={"dve": cpv}, inc=(xtsem[sl], 16))
                            tmst_rel[sl] = (xtsem[sl], K.c[xtsem[sl]])
                        stref[0] = ("mm", mm_last2)
                    else:
                        for dt_i in range(NDT):
                            stref[0] = ("xdma", K.emit("scalar", (lambda i, tt: lambda g: g.dma_start(
                                out=dst_xf[i * P:(i + 1) * P, tt:tt + TC],
                                in_=xc[:, xco + i * TC:xco + (i + 1) * TC]))(dt_i, t0),
                                waits={"dve": ev_end}, inc=("xdma", 16)))

                if tag == "A":
                    emit_cache(0)
                sct_list = [chunk_head(ch2) for ch2 in range(NCH)]
                if tag == "A":
                    emit_cache(1)
                for ch2 in range(NCH):
                    chunk_body(ch2, sct_list[ch2])
                if tag == "A":
                    CACHETOT.update({s: K.c[s] for s in ("wc0", "wc1", "wc2", "wc3")})
                store_ms = stref[0]
            K.barrier()
            return store_ms

        ffn1_store = ffn_call(inp, xfA, ("xdma", 0), "A", src_is_tm=True)

        with ExitStack() as ms:
            def msb(name, shape, dt):
                return ms.enter_context(nc.sbuf_tensor(name, shape, dt))
            mx = msb("mx", [P, NDT * S], F32)
            xn = msb("xn", [P, NDT * S], BF16)
            xnsA = msb("xnsA", [P, NDT], F32)
            xnsB = msb("xnsB", [P, NDT], F32)
            xns = msb("xns", [P, NDT], F32)
            ps_rel_dve = [0, 0, 0, 0]
            ps_rel_act = [0, 0, 0, 0]

            xld = 0
            for dt_i in range(NDT):
                xld = K.emit("scalar", (lambda i: lambda g: g.dma_start(
                    out=mx[:, i * S:(i + 1) * S], in_=xfA[i * P:(i + 1) * P, :]))(dt_i),
                    waits={ffn1_store[0]: ffn1_store[1]}, inc=("xdma", 16))
            ts2 = ExitStack()
            t_t = ts2.enter_context(nc.sbuf_tensor("t_t", [P, NDT * S], F32))
            mrowm = ts2.enter_context(nc.sbuf_tensor("mrowm", [1, S], F32))
            mrowr = ts2.enter_context(nc.sbuf_tensor("mrowr", [1, S], F32))
            mrows1 = ts2.enter_context(nc.sbuf_tensor("mrows1", [1, S], F32))
            bfscr2 = ts2.enter_context(nc.sbuf_tensor("bfscr2", [P, 2 * S], BF16))
            sq_ms = [0, 0]
            for dt_i in range(NDT):
                sl = dt_i % 2
                a1 = K.emit("scalar", (lambda i, q: lambda g: g.activation(
                    out=bfscr2[:, q * S:(q + 1) * S], in_=mx[:, i * S:(i + 1) * S],
                    func=AF.Copy))(dt_i, sl),
                    waits={"xdma": xld, "mm": sq_ms[sl]}, inc=("act", 1))
                mmv = 0
                for sc in range(4):
                    pj = 2 + sc // 2
                    mmv = K.emit("tensor", (lambda q, s_, j, i: lambda g: g.matmul(
                        out=ps[j][0:1, (s_ % 2) * 512:(s_ % 2 + 1) * 512], lhsT=onescb[:, :],
                        rhs=bfscr2[:, q * S + s_ * 512:q * S + (s_ + 1) * 512],
                        start=(i == 0), stop=(i == NDT - 1)))(sl, sc, pj, dt_i),
                        waits={"act": a1}, inc=("mm", 1) if sc == 3 else None)
                a2 = K.emit("scalar", (lambda i, q: lambda g: g.activation(
                    out=bfscr2[:, q * S:(q + 1) * S], in_=mx[:, i * S:(i + 1) * S],
                    func=AF.Square))(dt_i, sl),
                    waits={"mm": mmv}, inc=("act", 1))
                for sc in range(4):
                    pj = sc // 2
                    mmv = K.emit("tensor", (lambda q, s_, j, i: lambda g: g.matmul(
                        out=ps[j][0:1, (s_ % 2) * 512:(s_ % 2 + 1) * 512], lhsT=onescb[:, :],
                        rhs=bfscr2[:, q * S + s_ * 512:q * S + (s_ + 1) * 512],
                        start=(i == 0), stop=(i == NDT - 1)))(sl, sc, pj, dt_i),
                        waits={"act": a2}, inc=("mm", 1) if sc == 3 else None)
                sq_ms[sl] = mmv
            m_sums = K.c["mm"]
            dv = 0
            for hf in range(2):
                dv = K.emit("vector", (lambda h2: lambda g: g.tensor_scalar(
                    out=mrowm[0:1, h2 * 1024:(h2 + 1) * 1024], in0=ps[2 + h2][0:1, 0:1024],
                    scalar1=1.0 / D, scalar2=None, op0=ALU.mult))(hf),
                    waits={"mm": m_sums}, inc=("dve", 1))
            av = K.emit("scalar", lambda g: g.activation(out=mrows1[:, :], in_=mrowm[:, :],
                                                        func=AF.Square),
                        waits={"dve": dv}, inc=("act", 1))
            for hf in range(2):
                dv = K.emit("vector", (lambda h2: lambda g: g.scalar_tensor_tensor(
                    out=mrows1[0:1, h2 * 1024:(h2 + 1) * 1024], in0=ps[h2][0:1, 0:1024],
                    scalar=1.0 / D, in1=mrows1[0:1, h2 * 1024:(h2 + 1) * 1024],
                    op0=ALU.mult, op1=ALU.subtract))(hf),
                    waits={"act": av}, inc=("dve", 1))
            av = K.emit("scalar", lambda g: g.activation(out=mrowr[:, :], in_=mrows1[:, :],
                                                        func=AF.Sqrt, bias=1e-5),
                        waits={"dve": dv}, inc=("act", 1))
            d_rows = K.emit("vector", lambda g: g.reciprocal(mrows1[:, :], mrowr[:, :]),
                            waits={"act": av}, inc=("dve", 1))
            m_bc = 0
            for sc in range(4):
                K.emit("tensor", (lambda s_: lambda g: g.matmul(
                    out=ps[s_ // 2][:, (s_ % 2) * 512:(s_ % 2 + 1) * 512], lhsT=onesr[:, :],
                    rhs=mrowm[0:1, s_ * 512:(s_ + 1) * 512], start=True, stop=True))(sc),
                    waits={"dve": d_rows})
                m_bc = K.emit("tensor", (lambda s_: lambda g: g.matmul(
                    out=ps[2 + s_ // 2][:, (s_ % 2) * 512:(s_ % 2 + 1) * 512], lhsT=onesr[:, :],
                    rhs=mrows1[0:1, s_ * 512:(s_ + 1) * 512], start=True, stop=True))(sc),
                    waits={"dve": d_rows}, inc=("mm", 1))
            t_ms = 0
            for dt_i in range(NDT):
                for hf in range(2):
                    K.emit("vector", (lambda i, h2: lambda g: g.tensor_tensor(
                        out=t_t[:, i * S + h2 * 1024:i * S + (h2 + 1) * 1024],
                        in0=mx[:, i * S + h2 * 1024:i * S + (h2 + 1) * 1024],
                        in1=ps[h2][:, 0:1024], op=ALU.subtract))(dt_i, hf),
                        waits={"mm": m_bc}, inc=("dve", 1))
                    t_ms = K.emit("vector", (lambda i, h2: lambda g: g.tensor_tensor(
                        out=t_t[:, i * S + h2 * 1024:i * S + (h2 + 1) * 1024],
                        in0=t_t[:, i * S + h2 * 1024:i * S + (h2 + 1) * 1024],
                        in1=ps[2 + h2][:, 0:1024], op=ALU.mult))(dt_i, hf), inc=("dve", 1))
            sq_ms = [0, 0]
            mmv = 0
            for dt_i in range(NDT):
                sl = dt_i % 2
                a1 = K.emit("scalar", (lambda i, q: lambda g: g.activation(
                    out=bfscr2[:, q * S:(q + 1) * S], in_=t_t[:, i * S:(i + 1) * S],
                    func=AF.Square))(dt_i, sl),
                    waits={"dve": t_ms, "mm": sq_ms[sl]}, inc=("act", 1))
                for sc in range(4):
                    pj = sc // 2
                    w = {"act": a1}
                    if dt_i == 0 and sc == 0:
                        w["dve"] = t_ms
                    mmv = K.emit("tensor", (lambda q, s_, j, i: lambda g: g.matmul(
                        out=ps[j][0:1, (s_ % 2) * 512:(s_ % 2 + 1) * 512], lhsT=onescb[:, :],
                        rhs=bfscr2[:, q * S + s_ * 512:q * S + (s_ + 1) * 512],
                        start=(i == 0), stop=(i == NDT - 1)))(sl, sc, pj, dt_i),
                        waits=w, inc=("mm", 1) if sc == 3 else None)
                sq_ms[sl] = mmv
            m_tsq = K.c["mm"]
            dv = 0
            for hf in range(2):
                dv = K.emit("vector", (lambda h2: lambda g: g.tensor_scalar(
                    out=mrowr[0:1, h2 * 1024:(h2 + 1) * 1024], in0=ps[h2][0:1, 0:1024],
                    scalar1=1.0 / D, scalar2=None, op0=ALU.mult))(hf),
                    waits={"mm": m_tsq}, inc=("dve", 1))
            av = K.emit("scalar", lambda g: g.activation(out=mrowm[:, :], in_=mrowr[:, :],
                                                        func=AF.Sqrt, bias=1e-6),
                        waits={"dve": dv}, inc=("act", 1))
            dv = K.emit("vector", lambda g: g.reciprocal(mrowr[:, :], mrowm[:, :]),
                        waits={"act": av}, inc=("dve", 1))
            m_rb = 0
            for sc in range(4):
                m_rb = K.emit("tensor", (lambda s_: lambda g: g.matmul(
                    out=ps[s_ // 2][:, (s_ % 2) * 512:(s_ % 2 + 1) * 512], lhsT=onesr[:, :],
                    rhs=mrowr[0:1, s_ * 512:(s_ + 1) * 512], start=True, stop=True))(sc),
                    waits={"dve": dv}, inc=("mm", 1))
            xn_ms = 0
            for dt_i in range(NDT):
                for hf in range(2):
                    acc = xnsA if hf == 0 else xnsB
                    xn_ms = K.emit("vector", (lambda i, h2, a3: lambda g: g.scalar_tensor_tensor(
                        out=xn[:, i * S + h2 * 1024:i * S + (h2 + 1) * 1024],
                        in0=t_t[:, i * S + h2 * 1024:i * S + (h2 + 1) * 1024],
                        scalar=nw8[:, i:i + 1], in1=ps[h2][:, 0:1024],
                        op0=ALU.mult, op1=ALU.mult, accum_out=a3[:, i:i + 1]))(dt_i, hf, acc),
                        waits={"mm": m_rb}, inc=("dve", 1))
            d_xnsum = K.emit("vector", lambda g: g.tensor_tensor(
                out=xns[:, :], in0=xnsA[:, :], in1=xnsB[:, :], op=ALU.add), inc=("dve", 1))
            ps_rel_dve = [xn_ms, xn_ms, t_ms, t_ms]
            for dt_i in range(NDT):
                K.emit("vector", (lambda i: lambda g: g.tensor_tensor(
                    out=mx[:, i * S:(i + 1) * S], in0=mx[:, i * S:(i + 1) * S],
                    in1=t_t[:, i * S:(i + 1) * S], op=ALU.add))(dt_i), inc=("dve", 1))
            ts2.close()
            mmv = 0
            for dt_i in range(NDT):
                mmv = K.emit("tensor", (lambda i: lambda g: g.matmul(
                    out=ps[0][0:1, 0:8], lhsT=xns[:, i:i + 1], rhs=rwT[:, i * 8:(i + 1) * 8],
                    start=(i == 0), stop=(i == NDT - 1)))(dt_i),
                    waits={"dve": max(d_xnsum, ps_rel_dve[0])},
                    inc=("mm", 1) if dt_i == NDT - 1 else None)
            K.emit("vector", lambda g: g.tensor_scalar(
                out=r8a[:, :], in0=ps[0][0:1, 0:8], scalar1=1.0 / S, scalar2=None, op0=ALU.mult),
                waits={"mm": mmv}, inc=("dve", 1))
            K.emit("vector", lambda g: g.tensor_tensor(out=r8a[:, :], in0=r8a[:, :],
                                                       in1=rbrow[:, :], op=ALU.add), inc=("dve", 1))
            K.emit("vector", lambda g: g.reduce_max(r8b[0:1, 0:1], r8a[:, :], axis=AX),
                   inc=("dve", 1))
            dv = K.emit("vector", lambda g: g.tensor_scalar(
                out=r8b[0:1, 1:2], in0=r8b[0:1, 0:1], scalar1=-1.0, scalar2=None, op0=ALU.mult),
                inc=("dve", 1))
            av = K.emit("scalar", lambda g: g.activation(out=probrow[:, :], in_=r8a[:, :],
                                                        func=AF.Exp, bias=r8b[0:1, 1:2]),
                        waits={"dve": dv}, inc=("act", 1))
            K.emit("vector", lambda g: g.reduce_sum(r8b[0:1, 2:3], probrow[:, :], axis=AX),
                   waits={"act": av}, inc=("dve", 1))
            K.emit("vector", lambda g: g.reciprocal(r8b[0:1, 3:4], r8b[0:1, 2:3]), inc=("dve", 1))
            dv = K.emit("vector", lambda g: g.tensor_scalar(
                out=probrow[:, :], in0=probrow[:, :], scalar1=r8b[0:1, 3:4], scalar2=None,
                op0=ALU.mult), inc=("dve", 1))
            mmv = K.emit("tensor", lambda g: g.transpose(out=ps[1][0:8, 0:1], in_=probrow[:, :],
                                                        identity=ident[0:1, 0:1]),
                         waits={"dve": max(dv, ps_rel_dve[1])}, inc=("mm", 1))
            dvp = K.emit("vector", lambda g: g.tensor_copy(tiny[:, :], ps[1][0:8, 0:1]),
                         waits={"mm": mmv}, inc=("dve", 1))
            for e in range(E):
                mmv = K.emit("tensor", (lambda ee: lambda g: g.matmul(
                    out=ps[1][:, 8:9], lhsT=esel[0:E, ee * P:(ee + 1) * P], rhs=tiny[:, :],
                    start=True, stop=True))(e),
                    waits={"dve": K.c["dve"]}, inc=("mm", 1))
                K.emit("vector", (lambda ee: lambda g: g.tensor_copy(
                    pcols[:, ee:ee + 1], ps[1][:, 8:9]))(e),
                    waits={"mm": mmv}, inc=("dve", 1))
            d_pcols = K.c["dve"]

            with ExitStack() as hs2:
                hbf = hs2.enter_context(nc.sbuf_tensor("hbf", [P, NDT * S], BF16))
                hid_t = hs2.enter_context(nc.sbuf_tensor("hid_t", [P, S], F32))
                gat_t = hs2.enter_context(nc.sbuf_tensor("gat_t", [P, S], F32))
                ct_t = hs2.enter_context(nc.sbuf_tensor("ct_t", [P, S], F32))
                macc = hs2.enter_context(nc.sbuf_tensor("macc", [P, 4 * 1024], F32))
                wtT = hs2.enter_context(nc.sbuf_tensor("wtT", [P, NDT * 256], BF16))
                wotT = hs2.enter_context(nc.sbuf_tensor("wotT", [P, NDT * P], BF16))
                wst = hs2.enter_context(nc.sbuf_tensor("wst", [P, 5 * 1024], F32))
                wst_rel = [("dve", d_pcols)] * 5
                wmsem = ["wm0", "wm1", "wm2", "wm3", "wm4"]
                state = {"slot": 0, "hg_done": 0}
                macc_rel = [0, 0]

                def hg_fma(cq):
                    par = cq % 2
                    fma = 0
                    for e in range(E):
                        for part in range(2):
                            slq = state["slot"] % 5; state["slot"] += 1
                            roff = cq * P if part == 0 else D + cq * P
                            wld = K.emit("sync", (lambda ee, ro, q: lambda g: g.dma_start(
                                out=wst[:, q * 1024:(q + 1) * 1024],
                                in_=whg_d[ee, ro:ro + P, :]))(e, roff, slq),
                                waits={wst_rel[slq][0]: wst_rel[slq][1]}, inc=(wmsem[slq], 16))
                            tgt = par * 2048 + (0 if part == 0 else 1024)
                            if e == 0:
                                fma = K.emit("vector", (lambda q, tg, ee: lambda g: g.tensor_scalar(
                                    out=macc[:, tg:tg + 1024], in0=wst[:, q * 1024:(q + 1) * 1024],
                                    scalar1=pcols[:, ee:ee + 1], scalar2=None, op0=ALU.mult))(slq, tgt, e),
                                    waits={wmsem[slq]: wld, "dve": d_pcols, "mm": macc_rel[par]},
                                    inc=("dve", 1))
                            else:
                                fma = K.emit("vector", (lambda q, tg, ee: lambda g: g.scalar_tensor_tensor(
                                    out=macc[:, tg:tg + 1024], in0=wst[:, q * 1024:(q + 1) * 1024],
                                    scalar=pcols[:, ee:ee + 1], in1=macc[:, tg:tg + 1024],
                                    op0=ALU.mult, op1=ALU.add))(slq, tgt, e),
                                    waits={wmsem[slq]: wld}, inc=("dve", 1))
                            wst_rel[slq] = ("dve", fma)
                    return fma

                def hg_tp(cq, fma):
                    par = cq % 2
                    a_tp = 0
                    mm_last = 0
                    for part in range(2):
                        src_off = par * 2048 + (0 if part == 0 else 1024)
                        pj = 2 + part
                        mmv = 0
                        for dt_i in range(NDT):
                            mmv = K.emit("tensor", (lambda so, i, j: lambda g: g.transpose(
                                out=ps[j][:, i * P:(i + 1) * P],
                                in_=macc[:, so + i * P:so + (i + 1) * P],
                                identity=ident[:, :]))(src_off, dt_i, pj),
                                waits={"dve": fma, "act": ps_rel_act[pj]},
                                inc=("mm", 1) if dt_i == NDT - 1 else None)
                        mm_last = mmv
                        a_tp = K.emit("scalar", (lambda j, pt: lambda g: g.activation(
                            out=wtT[:, :].rearrange("p (dt two) -> p dt two", two=256)[:, :, pt * P:(pt + 1) * P],
                            in_=ps[j][:, :].rearrange("p (dt c) -> p dt c", c=P),
                            func=AF.Copy))(pj, part),
                            waits={"mm": mmv}, inc=("act", 1))
                        ps_rel_act[pj] = a_tp
                    macc_rel[par] = mm_last
                    return a_tp

                def hg_mmscan(cq, a_tp):
                    prev_done = state["hg_done"]
                    mm_hid = 0; mm_gat = 0
                    for part in range(2):
                        for sc in range(4):
                            pj = (0 if part == 0 else 2) + sc // 2
                            mmv = 0
                            for dt_i in range(NDT):
                                w = {"act": max(a_tp, ps_rel_act[pj])} if dt_i == 0 else {"act": a_tp}
                                if dt_i == 0:
                                    w["dve"] = ps_rel_dve[pj]
                                mmv = K.emit("tensor", (lambda pt, s_, j, i: lambda g: g.matmul(
                                    out=ps[j][:, (s_ % 2) * 512:(s_ % 2 + 1) * 512],
                                    lhsT=wtT[:, i * 256 + pt * P:i * 256 + (pt + 1) * P],
                                    rhs=xn[:, i * S + s_ * 512:i * S + (s_ + 1) * 512],
                                    start=(i == 0), stop=(i == NDT - 1)))(part, sc, pj, dt_i),
                                    waits=w, inc=("mm", 1) if dt_i == NDT - 1 else None)
                            if part == 0:
                                mm_hid = mmv
                            else:
                                mm_gat = mmv
                    a_h = 0; a_g = 0
                    for part in range(2):
                        for hf in range(2):
                            pj = (0 if part == 0 else 2) + hf
                            dst = hid_t if part == 0 else gat_t
                            av2 = K.emit("scalar", (lambda j, d3, h2: lambda g: g.activation(
                                out=d3[:, h2 * 1024:(h2 + 1) * 1024], in_=ps[j][:, 0:1024],
                                func=AF.Copy))(pj, dst, hf),
                                waits={"mm": (mm_hid if part == 0 else mm_gat),
                                       "dve": prev_done}, inc=("act", 1))
                            ps_rel_act[pj] = av2
                            if part == 0:
                                a_h = av2
                            else:
                                a_g = av2
                    a_c = K.emit("scalar", lambda g: g.activation(out=ct_t[:, :], in_=gat_t[:, :],
                                                                 func=AF.Sigmoid, scale=-1.0),
                                 waits={"act": a_g, "dve": prev_done}, inc=("act", 1))
                    a_z = 0
                    for hf2 in range(2):
                        a_z = K.emit("scalar", (lambda h3: lambda g: g.activation(
                            out=ps[h3][:, 0:1024], in_=gat_t[:, h3 * 1024:(h3 + 1) * 1024],
                            func=AF.Sigmoid))(hf2), inc=("act", 1))
                    a_s = K.emit("scalar", lambda g: g.activation(out=gat_t[:, :], in_=hid_t[:, :],
                                                                 func=AF.Sigmoid),
                                 waits={"act": a_h}, inc=("act", 1))
                    K.emit("vector", lambda g: g.tensor_scalar(out=hid_t[:, :], in0=hid_t[:, :],
                                                               scalar1=0.5, scalar2=None, op0=ALU.add),
                           waits={"act": a_s}, inc=("dve", 1))
                    K.emit("vector", lambda g: g.tensor_tensor(out=hid_t[:, :], in0=hid_t[:, :],
                                                               in1=gat_t[:, :], op=ALU.max),
                           inc=("dve", 1))
                    dvv2 = 0
                    for hf2 in range(2):
                        dvv2 = K.emit("vector", (lambda h3: lambda g: g.tensor_tensor(
                            out=hid_t[:, h3 * 1024:(h3 + 1) * 1024],
                            in0=hid_t[:, h3 * 1024:(h3 + 1) * 1024],
                            in1=ps[h3][:, 0:1024], op=ALU.mult))(hf2),
                            waits={"act": a_z}, inc=("dve", 1))
                    ps_rel_dve[0] = dvv2; ps_rel_dve[1] = dvv2
                    K.emit("vector", lambda g: g.tensor_tensor_scan(
                        out=gat_t[:, :], data0=ct_t[:, :], data1=hid_t[:, :], initial=0.0,
                        op0=ALU.mult, op1=ALU.add),
                        waits={"act": a_c}, inc=("dve", 1))
                    state["hg_done"] = K.emit("vector", (lambda c2: lambda g: g.tensor_copy(
                        hbf[:, c2 * S:(c2 + 1) * S], gat_t[:, :]))(cq), inc=("dve", 1))

                fma = hg_fma(0)
                a_tp = hg_tp(0, fma)
                for cq in range(NDT):
                    if cq + 1 < NDT:
                        fma_n = hg_fma(cq + 1)
                    hg_mmscan(cq, a_tp)
                    if cq + 1 < NDT:
                        a_tp = hg_tp(cq + 1, fma_n)

                def o_fma(oq):
                    par = oq % 2
                    fma = 0
                    for e in range(E):
                        slq = state["slot"] % 5; state["slot"] += 1
                        wld = K.emit("sync", (lambda ee, ro, q: lambda g: g.dma_start(
                            out=wst[:, q * 1024:(q + 1) * 1024],
                            in_=wout_d[ee, ro:ro + P, :]))(e, oq * P, slq),
                            waits={wst_rel[slq][0]: wst_rel[slq][1]}, inc=(wmsem[slq], 16))
                        tgt = par * 2048
                        if e == 0:
                            fma = K.emit("vector", (lambda q, tg, ee: lambda g: g.tensor_scalar(
                                out=macc[:, tg:tg + 1024], in0=wst[:, q * 1024:(q + 1) * 1024],
                                scalar1=pcols[:, ee:ee + 1], scalar2=None, op0=ALU.mult))(slq, tgt, e),
                                waits={wmsem[slq]: wld, "mm": macc_rel[par]}, inc=("dve", 1))
                        else:
                            fma = K.emit("vector", (lambda q, tg, ee: lambda g: g.scalar_tensor_tensor(
                                out=macc[:, tg:tg + 1024], in0=wst[:, q * 1024:(q + 1) * 1024],
                                scalar=pcols[:, ee:ee + 1], in1=macc[:, tg:tg + 1024],
                                op0=ALU.mult, op1=ALU.add))(slq, tgt, e),
                                waits={wmsem[slq]: wld}, inc=("dve", 1))
                        wst_rel[slq] = ("dve", fma)
                    return fma

                def o_tp(oq, fma):
                    par = oq % 2
                    pj = 2 + par
                    mmv = 0
                    for kt in range(NDT):
                        mmv = K.emit("tensor", (lambda i, j, so: lambda g: g.transpose(
                            out=ps[j][:, i * P:(i + 1) * P], in_=macc[:, so + i * P:so + (i + 1) * P],
                            identity=ident[:, :]))(kt, pj, par * 2048),
                            waits={"dve": fma, "act": ps_rel_act[pj]},
                            inc=("mm", 1) if kt == NDT - 1 else None)
                    macc_rel[par] = mmv
                    a_tp = K.emit("scalar", (lambda j: lambda g: g.activation(
                        out=wotT[:, :], in_=ps[j][:, :], func=AF.Copy))(pj),
                        waits={"mm": mmv}, inc=("act", 1))
                    ps_rel_act[pj] = a_tp
                    return a_tp

                def o_mm(oq, a_tp):
                    for hf in range(2):
                        pj = hf
                        mmv = 0
                        for sc in range(2):
                            for kt in range(NDT):
                                w = {"act": a_tp, "dve": state["hg_done"]}
                                if kt == 0 and sc == 0:
                                    w["dve"] = max(state["hg_done"], ps_rel_dve[pj])
                                mmv = K.emit("tensor", (lambda h2, s_, j, i: lambda g: g.matmul(
                                    out=ps[j][:, s_ * 512:(s_ + 1) * 512],
                                    lhsT=wotT[:, i * P:(i + 1) * P],
                                    rhs=hbf[:, i * S + h2 * 1024 + s_ * 512:
                                            i * S + h2 * 1024 + (s_ + 1) * 512],
                                    start=(i == 0), stop=(i == NDT - 1)))(hf, sc, pj, kt),
                                    waits=w,
                                    inc=("mm", 1) if (kt == NDT - 1 and sc == 1) else None)
                        dvv = K.emit("vector", (lambda o, h2, j: lambda g: g.tensor_tensor(
                            out=mx[:, o * S + h2 * 1024:o * S + (h2 + 1) * 1024],
                            in0=mx[:, o * S + h2 * 1024:o * S + (h2 + 1) * 1024],
                            in1=ps[j][:, 0:1024], op=ALU.add))(oq, hf, pj),
                            waits={"mm": mmv}, inc=("dve", 1))
                        ps_rel_dve[pj] = dvv

                fma = o_fma(0)
                a_tp = o_tp(0, fma)
                for oq in range(NDT):
                    if oq + 1 < NDT:
                        fma_n = o_fma(oq + 1)
                    o_mm(oq, a_tp)
                    if oq + 1 < NDT:
                        a_tp = o_tp(oq + 1, fma_n)
            ev_end = K.c["dve"]
            mix_store = 0
            for dt_i in range(NDT):
                mix_store = K.emit("scalar", (lambda i: lambda g: g.dma_start(
                    out=xfB[i * P:(i + 1) * P, :], in_=mx[:, i * S:(i + 1) * S]))(dt_i),
                    waits={"dve": ev_end}, inc=("xdma", 16))
        K.barrier()

        ffn2_store = ffn_call(xfB, out_d, ("xdma", mix_store), "B", dst_is_tm=True)

        K.emit("sync", lambda g: g.nop(), waits={"xdma": K.c["xdma"], "xt0": K.c["xt0"], "xt1": K.c["xt1"]})

        with nc.Block() as block:
            def replay(name):
                def _(eng):
                    waited = {}
                    for w, inc, fn in K.ops[name]:
                        for s2, v in w.items():
                            if v > waited.get(s2, 0):
                                eng.wait_ge(sems[s2], v)
                                waited[s2] = v
                        r = fn(eng)
                        if inc:
                            r.then_inc(sems[inc[0]], inc[1])
                return _
            block.tensor(replay("tensor"))
            block.vector(replay("vector"))
            block.scalar(replay("scalar"))
            block.gpsimd(replay("gpsimd"))
            block.sync(replay("sync"))

    return nc


_AUX = None


def _aux():
    global _AUX
    if _AUX is None:
        esel = np.zeros((E, E * P), dtype=np.float32)
        for e in range(E):
            esel[e, e * P:(e + 1) * P] = 1.0
        _AUX = {
            "ident": np.eye(P, dtype=np.float32),
            "esel": esel,
            "onesr": np.ones((1, P), dtype=np.float32),
            "onescf": np.ones((P, 1), dtype=np.float32),
            "cvals": np.tile(np.array([[1e-5, 1e-6]], dtype=np.float32), (P, 1)),
            "ecolh": np.broadcast_to((np.arange(E, dtype=np.float32) * CAP - 1)[None, :], (P, E)).copy(),
            "eclimh": np.broadcast_to((np.arange(E, dtype=np.float32) * CAP + CAP - 1)[None, :], (P, E)).copy(),
        }
    return _AUX


_NC_CACHE = None


def kernel(L0_attn_inputs, L0_attn_norm_weight, L0_attn_router_w, L0_attn_router_b,
           L0_attn_w_hg, L0_attn_w_out, L0_ffn_w_router, L0_ffn_w1, L0_ffn_b1,
           L0_ffn_w2, L0_ffn_b2):
    global _NC_CACHE
    if _NC_CACHE is None:
        _NC_CACHE = build_program(debug=False)
    nc = _NC_CACHE
    aux = _aux()
    f32 = lambda a: np.ascontiguousarray(a, dtype=np.float32)
    nw = f32(L0_attn_norm_weight)
    wr = f32(L0_ffn_w_router)
    b1 = f32(L0_ffn_b1)
    b2 = f32(L0_ffn_b2)
    common = {
        "rw": f32(L0_attn_router_w),
        "rb": f32(L0_attn_router_b).reshape(1, E),
        "whg": f32(L0_attn_w_hg),
        "wout": f32(L0_attn_w_out),
        "w1": f32(L0_ffn_w1),
        "w2": f32(L0_ffn_w2),
        "nw8h": np.ascontiguousarray(nw.reshape(NDT, P).T),
        "wr64h": np.ascontiguousarray(wr.reshape(NDT, P, E).transpose(1, 0, 2).reshape(P, NDT * E)),
        "b1ah": np.ascontiguousarray(b1.reshape(E, NJC, P).transpose(2, 0, 1).reshape(P, E * NJC)),
        "b2ah": np.ascontiguousarray(b2.reshape(E, NDT, P).transpose(2, 0, 1).reshape(P, E * NDT)),
        **aux,
    }
    x = f32(L0_attn_inputs)
    in_maps = [dict(common, inp=np.ascontiguousarray(x[b])) for b in range(8)]
    res = run_bass_kernel_spmd(nc, in_maps, core_ids=list(range(8)))
    return np.stack([r["out"] for r in res.results], axis=0)
